# version 06
# baseline (speedup 1.0000x reference)
import numpy as np
import ml_dtypes
import concourse.bass as bass
import concourse.mybir as mybir
from concourse.bass_utils import run_bass_kernel_spmd

F32 = mybir.dt.float32
BF16 = mybir.dt.bfloat16
AF = mybir.ActivationFunctionType
ALU = mybir.AluOpType

S = 2048
NC = 8
BF = ml_dtypes.bfloat16


def _img_lhsT(W):
    K, M = W.shape
    assert K % 128 == 0 and M % 128 == 0, (K, M)
    nk, nm = K // 128, M // 128
    W4 = W.reshape(nk, 128, nm, 128)
    return np.ascontiguousarray(W4.transpose(2, 1, 0, 3).reshape(nm, 128, nk * 128))


def _bias_cols(v):
    n = v.shape[0] // 128
    return np.ascontiguousarray(v.reshape(n, 128).T)



class P:

    def __init__(self):
        self.nc = bass.Bass("TRN2", target_bir_lowering=False)
        self.ctx = []
        self.sem_ctx = []
        self.sems = {}

    def dram(self, name, shape, dtype, kind):
        return self.nc.dram_tensor(name, list(shape), dtype, kind=kind).ap()

    def sbuf(self, name, shape, dtype):
        cm = self.nc.sbuf_tensor(name, list(shape), dtype)
        t = cm.__enter__()
        self.ctx.append(cm)
        return t

    def psum(self, shape, dtype=F32):
        cm = self.nc.psum_tensor(list(shape), dtype)
        t = cm.__enter__()
        self.ctx.append(cm)
        return t

    def sem(self, name):
        cm = self.nc.semaphore(name)
        h = cm.__enter__()
        self.sem_ctx.append(cm)
        self.sems[name] = [h, 0]
        return name

    def scope(self):
        p = self

        class _Scope:
            def __enter__(self):
                self.n0 = len(p.ctx)
                return self

            def __exit__(self, *a):
                for cm in reversed(p.ctx[self.n0:]):
                    cm.__exit__(None, None, None)
                del p.ctx[self.n0:]
                return False
        return _Scope()

    def barrier(self, bar):
        nc = self.nc
        engs = [nc.gpsimd, nc.vector, nc.tensor, nc.scalar, nc.sync]
        for e in engs:
            e.drain()
            self.inc(e.nop(), bar, 1)
        tgt = self.val(bar)
        for e in engs:
            e.wait_ge(self.sems[bar][0], tgt)

    def inc(self, inst, name, amt=1):
        s = self.sems[name]
        inst.then_inc(s[0], amt)
        s[1] += amt
        return s[1]

    def val(self, name):
        return self.sems[name][1]

    def wait(self, eng, name, val=None):
        s = self.sems[name]
        eng.wait_ge(s[0], s[1] if val is None else val)

    def close(self):
        for cm in reversed(self.ctx):
            cm.__exit__(None, None, None)
        for cm in reversed(self.sem_ctx):
            cm.__exit__(None, None, None)
        self.ctx = []
        self.sem_ctx = []

    @property
    def e(self):
        nc = self.nc
        return dict(gpsimd=nc.gpsimd, vector=nc.vector, tensor=nc.tensor,
                    scalar=nc.scalar, sync=nc.sync)


def load_consts(p, io):
    nc = p.nc
    g = nc.gpsimd
    ld = p.sem("c_ld")
    specs = [
        ("ones_bf", [128, 128], BF16), ("ones_f32", [128, 128], F32),
        ("ident_bf", [128, 128], BF16), ("ident_f32", [128, 128], F32),
        ("triu01", [128, 128], BF16),
    ]
    for name, shape, dt in specs:
        ap = p.dram("c_" + name, shape, dt, "ExternalInput")
        sb = p.sbuf("c_" + name + "_sb", shape, dt)
        p.inc(g.dma_start(out=sb[:, :], in_=ap), ld, 16)
        io[name + "_sb"] = sb
    io["sel_ap"] = p.dram("c_sel", [64, 32 * 128], BF16, "ExternalInput")
    io["sel64_ap"] = p.dram("c_sel64", [64, 4096], BF16, "ExternalInput")
    io["c_ld"] = ld
    return ld


def consts_host():
    ones_bf = np.ones((128, 128), dtype=BF)
    ones_f32 = np.ones((128, 128), dtype=np.float32)
    ident_bf = np.eye(128, dtype=BF)
    ident_f32 = np.eye(128, dtype=np.float32)
    sl = np.arange(128)
    triu01 = (sl[:, None] <= sl[None, :]).astype(BF)
    sel = np.zeros((32, 64, 128), dtype=BF)
    for j in range(32):
        for m in range(128):
            sel[j, (128 * j + m) // 64, m] = 1
    sel = np.ascontiguousarray(sel.transpose(1, 0, 2).reshape(64, 32 * 128))
    sel64 = np.zeros((64, 4096), dtype=BF)
    for h in range(64):
        sel64[h, h * 64:(h + 1) * 64] = 1
    return {"c_ones_bf": ones_bf, "c_ones_f32": ones_f32, "c_ident_bf": ident_bf,
            "c_ident_f32": ident_f32, "c_triu01": triu01, "c_sel": sel,
            "c_sel64": sel64}



def emit_ln_router(p, io, sfx, src_ap, xn_ap, router=False, wr_img_ap=None, w_ap=None):
    nc = p.nc
    g, v, t, s_, sy = nc.gpsimd, nc.vector, nc.tensor, nc.scalar, nc.sync
    DT = 16
    W = 2048

    x_sb = p.sbuf(sfx + "_x", [128, 2 * W], F32)
    xb_sb = p.sbuf(sfx + "_xb", [128, DT * W], BF16)
    sq_t = p.sbuf(sfx + "_sq", [128, 2 * W], BF16)
    rb_t = p.sbuf(sfx + "_rb", [128, 2 * W], BF16)
    rows = p.sbuf(sfx + "_rows", [1, 4 * W], F32)
    invb = p.sbuf(sfx + "_invb", [128, W], BF16)
    minvb = p.sbuf(sfx + "_minvb", [128, W], BF16)
    xn_buf = p.sbuf(sfx + "_xnb", [128, 2 * W], BF16)
    tmp = p.sbuf(sfx + "_tmp", [128, 2 * W], BF16)
    ps_all = p.psum([128, W])
    ps_b = p.psum([128, 1024])
    if router:
        wr_sb = p.sbuf(sfx + "_wr", [128, 128], F32)
        csum8 = p.sbuf(sfx + "_csum8", [128, 8], F32)
        rstage = p.sbuf(sfx + "_rstage", [128, W], F32)
        tok = p.sbuf(sfx + "_tok", [128, 128], F32)
        t8 = p.sbuf(sfx + "_t8", [128, 64], F32)
        c1 = p.sbuf(sfx + "_c1", [128, 8], F32)
        wtr = p.sbuf(sfx + "_wtr", [128, 128], F32)
        w_sb = p.sbuf(sfx + "_wsb", [8, S], F32)
        csr = p.sbuf(sfx + "_csr", [1, 8], F32)
        ps_t = p.psum([128, 128])
        ps_c = p.psum([1, 128])

    ld = [p.sem(sfx + "_ld0"), p.sem(sfx + "_ld1")]
    wld = p.sem(sfx + "_wld")
    mm = p.sem(sfx + "_mm")
    ev = p.sem(sfx + "_ev")
    sc = p.sem(sfx + "_sc")
    st = [p.sem(sfx + "_st0"), p.sem(sfx + "_st1")]

    ones_bf = io["ones_bf_sb"]
    ones_f32 = io["ones_f32_sb"]
    ident_f32 = io["ident_f32_sb"]
    p.wait(g, io["c_ld"])

    if router:
        p.inc(g.dma_start(out=wr_sb[:, :], in_=wr_img_ap), wld, 16)
        p.wait(t, wld)
        i1 = t.matmul(ps_c[0:1, :], ones_f32[:, 0:1], wr_sb[:, :], start=True, stop=True)
        p.inc(i1, mm)
        p.wait(v, mm)
        i2 = v.tensor_reduce(csr[0:1, 0:8],
                             ps_c[0:1, :].rearrange("p (j e) -> p e j", e=8),
                             mybir.AxisListType.X, ALU.add)
        p.inc(i2, ev)
        p.wait(t, ev)
        i3 = t.matmul(ps_t[:, 0:8], ones_f32[0:1, :], csr[0:1, 0:8], start=True, stop=True)
        p.inc(i3, mm)
        p.wait(v, mm)
        p.inc(v.tensor_copy(csum8[:, :], ps_t[:, 0:8]), ev)

    mm_t = []
    for j in range(DT):
        par = j % 2
        if j >= 2:
            p.wait(sy, mm, mm_t[j - 2])
        p.inc(sy.dma_start(out=x_sb[:, par * W:(par + 1) * W],
                           in_=src_ap[j * 128:(j + 1) * 128, :]), ld[par], 16)
        p.wait(s_, ld[par], p.val(ld[par]))
        i1 = s_.copy(xb_sb[:, j * W:(j + 1) * W], x_sb[:, par * W:(par + 1) * W])
        p.inc(i1, sc)
        p.wait(v, sc)
        v.tensor_mul(sq_t[:, par * W:(par + 1) * W], xb_sb[:, j * W:(j + 1) * W],
                     xb_sb[:, j * W:(j + 1) * W])
        i2 = v.tensor_sub(rb_t[:, par * W:(par + 1) * W], x_sb[:, par * W:(par + 1) * W],
                          xb_sb[:, j * W:(j + 1) * W])
        p.inc(i2, ev)
        p.wait(t, ev)
        for nb in range(4):
            nsl = slice(nb * 512, (nb + 1) * 512)
            psl = slice(par * W + nb * 512, par * W + (nb + 1) * 512)
            xsl = slice(j * W + nb * 512, j * W + (nb + 1) * 512)
            t.matmul(ps_all[0:1, nsl], ones_bf[:, 0:1], xb_sb[:, xsl],
                     start=(j == 0), stop=False)
            t.matmul(ps_all[0:1, nsl], ones_bf[:, 0:1], rb_t[:, psl],
                     start=False, stop=(j == DT - 1))
            i3 = t.matmul(ps_all[32:33, nsl], ones_bf[:, 0:1], sq_t[:, psl],
                          start=(j == 0), stop=(j == DT - 1))
            if router:
                i3 = t.matmul(ps_all[64:72, nsl], wr_sb[:, j * 8:(j + 1) * 8],
                              x_sb[:, psl], start=(j == 0), stop=(j == DT - 1))
        p.inc(i3, mm)
        mm_t.append(p.val(mm))

    def R(k):
        return rows[0:1, k * W:(k + 1) * W]
    p.wait(v, mm)
    v.tensor_scalar_mul(R(0), ps_all[0:1, :], 1.0 / 2048.0)
    v.tensor_scalar_mul(R(1), ps_all[32:33, :], 1.0 / 2048.0)
    v.drain()
    v.tensor_mul(R(2), R(0), R(0))
    v.drain()
    v.tensor_sub(R(1), R(1), R(2))
    v.drain()
    i3 = v.tensor_scalar_add(R(1), R(1), 1e-5)
    p.inc(i3, ev)
    p.wait(s_, ev)
    s_.activation(R(2), R(1), AF.Ln)
    s_.drain()
    i4 = s_.activation(R(2), R(2), AF.Exp, scale=-0.5)
    p.inc(i4, sc)
    p.wait(v, sc)
    i5 = v.tensor_mul(R(3), R(0), R(2))
    p.inc(i5, ev)
    p.wait(t, ev)
    ev_b = []
    for nb in range(4):
        nsl = slice(nb * 512, (nb + 1) * 512)
        if nb >= 1:
            p.wait(t, ev, ev_b[-1])
        t.matmul(ps_b[:, 0:512], ones_f32[0:1, :], R(2)[0:1, nsl], start=True, stop=True)
        i6 = t.matmul(ps_b[:, 512:1024], ones_f32[0:1, :], R(3)[0:1, nsl], start=True, stop=True)
        p.inc(i6, mm)
        p.wait(v, mm)
        v.tensor_copy(invb[:, nsl], ps_b[:, 0:512])
        i7 = v.tensor_copy(minvb[:, nsl], ps_b[:, 512:1024])
        p.inc(i7, ev)
        ev_b.append(p.val(ev))

    for j in range(DT):
        par = j % 2
        if j >= 2:
            p.wait(v, st[par], p.val(st[par]))
        v.drain()
        v.tensor_mul(tmp[:, par * W:(par + 1) * W], xb_sb[:, j * W:(j + 1) * W], invb[:, :])
        v.drain()
        i8 = v.tensor_sub(xn_buf[:, par * W:(par + 1) * W], tmp[:, par * W:(par + 1) * W],
                          minvb[:, :])
        p.inc(i8, ev)
        p.wait(sy, ev)
        p.inc(sy.dma_start(out=xn_ap[j * 128:(j + 1) * 128, :],
                           in_=xn_buf[:, par * W:(par + 1) * W]), st[par], 16)

    if router:
        p.wait(v, mm, mm_t[-1])
        v.tensor_copy(rstage[0:8, :], ps_all[64:72, :])
        v.tensor_copy(rstage[32:33, :], R(0))
        i9 = v.tensor_copy(rstage[64:65, :], R(2))
        p.inc(i9, ev)
        for gi in range(W // 128):
            p.wait(t, ev)
            i10 = t.transpose(ps_t[:, :], rstage[:, gi * 128:(gi + 1) * 128], ident_f32[:, :])
            p.inc(i10, mm)
            p.wait(v, mm)
            p.inc(v.tensor_copy(tok[:, :], ps_t[:, :]), ev)
            lg = tok[:, 0:8]
            m_c = tok[:, 32:33]
            i_c = tok[:, 64:65]
            v.drain()
            v.scalar_tensor_tensor(t8[:, 0:8], csum8[:, :], m_c, lg, ALU.mult, ALU.subtract)
            v.drain()
            v.tensor_scalar(t8[:, 8:16], t8[:, 0:8], i_c, -1.0, ALU.mult, ALU.mult)
            v.drain()
            v.tensor_reduce(c1[:, 0:1], t8[:, 8:16], mybir.AxisListType.X, ALU.max)
            v.drain()
            i11 = v.tensor_scalar_mul(c1[:, 1:2], c1[:, 0:1], -1.0)
            p.inc(i11, ev)
            p.wait(s_, ev)
            i12 = s_.activation(t8[:, 16:24], t8[:, 8:16], AF.Exp, bias=c1[:, 1:2])
            p.inc(i12, sc)
            p.wait(v, sc)
            e8 = t8[:, 16:24]
            v.tensor_reduce(c1[:, 2:3], e8, mybir.AxisListType.X, ALU.max)
            v.drain()
            v.tensor_single_scalar(t8[:, 24:32], e8, c1[:, 2:3], ALU.is_ge)
            v.drain()
            v.tensor_mul(t8[:, 32:40], e8, t8[:, 24:32])
            v.drain()
            v.tensor_sub(t8[:, 40:48], e8, t8[:, 32:40])
            v.drain()
            v.tensor_reduce(c1[:, 3:4], t8[:, 40:48], mybir.AxisListType.X, ALU.max)
            v.drain()
            v.tensor_single_scalar(t8[:, 24:32], e8, c1[:, 3:4], ALU.is_ge)
            v.drain()
            v.tensor_mul(t8[:, 32:40], e8, t8[:, 24:32])
            v.drain()
            v.tensor_reduce(c1[:, 4:5], t8[:, 32:40], mybir.AxisListType.X, ALU.add)
            v.drain()
            v.reciprocal(c1[:, 5:6], c1[:, 4:5])
            v.drain()
            i13 = v.tensor_scalar_mul(wtr[:, 0:8], t8[:, 32:40], c1[:, 5:6])
            p.inc(i13, ev)
            p.wait(t, ev)
            i14 = t.transpose(ps_t[:, :], wtr[:, :], ident_f32[:, :])
            p.inc(i14, mm)
            p.wait(v, mm)
            p.inc(v.tensor_copy(w_sb[0:8, gi * 128:(gi + 1) * 128], ps_t[0:8, :]), ev)
        p.wait(sy, ev)
        p.inc(sy.dma_start(out=w_ap, in_=w_sb[:, :]), st[0], 16)
    p.wait(sy, st[0])
    p.wait(sy, st[1])



def build_s0(router=True, sfx="s0"):
    p = P()
    io = {}
    xT = p.dram("xT", [2048, S], F32, "ExternalInput")
    xn = p.dram("xn_bf", [2048, S], BF16, "ExternalOutput")
    wr_img = p.dram("wr_img", [128, 128], F32, "ExternalInput") if router else None
    w_fm = p.dram("w_fm", [8, S], F32, "ExternalOutput") if router else None
    load_consts(p, io)
    emit_ln_router(p, io, sfx, xT, xn, router=router, wr_img_ap=wr_img, w_ap=w_fm)
    p.close()
    return p


def host_wr_img(Wr):
    return np.ascontiguousarray(Wr.reshape(16, 128, 8).transpose(1, 0, 2).reshape(128, 128))






class Pool:

    def __init__(self, p, sfx, max_nk, cbw=512, dual=True, depth=3):
        self.p = p
        self.cbw = cbw
        self.max_nk = max_nk
        self.depth = depth
        self.nstreams = 2 if dual else 1
        self.wbuf = p.sbuf(sfx + "_wb", [128, depth * self.nstreams * max_nk * 128], BF16)
        self.ps_a = p.psum([128, 2 * cbw])
        self.ps_b = p.psum([128, 2 * cbw]) if dual else None
        self.ldw = [p.sem(sfx + "_ldw%d" % i) for i in range(depth)]
        self.mm = p.sem(sfx + "_mm")
        self.bk = p.sem(sfx + "_bk")
        self.n_group = 0
        self.bk_after = []
        self.wpar_last_group = [None] * depth
        self.wpar = 0

    def wv(self, par, wi):
        off = (par * self.nstreams + wi) * self.max_nk * 128
        return self.wbuf[:, off:off + self.max_nk * 128]


def mm_pass(p, pool, *, src_sb, src_cols, n_k, n_m, Wimg, evac, W2img=None,
            on_m=None, src2_sb=None, cbw=None, src_wait=None):
    nc = p.nc
    g, t = nc.gpsimd, nc.tensor
    cbw = cbw or pool.cbw
    ncb = src_cols // cbw
    nw = 2 if W2img is not None else 1
    assert n_k <= pool.max_nk and cbw <= pool.cbw
    if src_wait is not None:
        p.wait(nc.tensor, src_wait[0], src_wait[1])

    D = pool.depth
    for m in range(n_m + D - 1):
        if m < n_m:
            par = pool.wpar
            pool.wpar = (pool.wpar + 1) % D
            prev = pool.wpar_last_group[par]
            if prev is not None:
                p.wait(g, pool.mm, prev + 1)
            p.inc(g.dma_start(out=pool.wv(par, 0)[:, 0:n_k * 128],
                              in_=Wimg[m * 128:(m + 1) * 128, :]), pool.ldw[par], 16)
            if nw == 2:
                p.inc(g.dma_start(out=pool.wv(par, 1)[:, 0:n_k * 128],
                                  in_=W2img[m * 128:(m + 1) * 128, :]), pool.ldw[par], 16)
            pool._pending = getattr(pool, "_pending", {})
            pool._pending[m] = (par, p.val(pool.ldw[par]))
        if m < D - 1:
            continue
        mi = m - (D - 1)
        par, ldw_tgt = pool._pending.pop(mi)
        if on_m is not None:
            on_m(mi)
        p.wait(t, pool.ldw[par], ldw_tgt)
        for cb in range(ncb):
            n = pool.n_group
            b = n % 2
            if n >= 2:
                p.wait(t, pool.bk, pool.bk_after[n - 2])
            pa = pool.ps_a[:, b * cbw:(b + 1) * cbw]
            for k in range(n_k):
                i1 = t.matmul(pa, pool.wv(par, 0)[:, k * 128:(k + 1) * 128],
                              src_sb[:, k * src_cols + cb * cbw:k * src_cols + (cb + 1) * cbw],
                              start=(k == 0), stop=(k == n_k - 1))
            pb = None
            if nw == 2:
                pb = pool.ps_b[:, b * cbw:(b + 1) * cbw]
                s2 = src2_sb if src2_sb is not None else src_sb
                for k in range(n_k):
                    i1 = t.matmul(pb, pool.wv(par, 1)[:, k * 128:(k + 1) * 128],
                                  s2[:, k * src_cols + cb * cbw:k * src_cols + (cb + 1) * cbw],
                                  start=(k == 0), stop=(k == n_k - 1))
            p.inc(i1, pool.mm)
            pool.n_group += 1
            pool.wpar_last_group[par] = pool.n_group - 1
            ninc = evac(mi, cb, pa, pb)
            pool.bk_after.append(p.val(pool.bk))
            assert ninc >= 1



def emit_l0_attn(p, io, sfx, xn_ap, xT_ap, g_ap, x1_ap, imgs, bias_aps):
    nc = p.nc
    g_, v, t, s_, sy = nc.gpsimd, nc.vector, nc.tensor, nc.scalar, nc.sync
    H = 1024
    DT = 16
    CB = 256

    xn_sb = p.sbuf(sfx + "_xn", [128, DT * H], BF16)
    d_sb = p.sbuf(sfx + "_d", [128, DT * H], BF16)
    u_sb = p.sbuf(sfx + "_u", [128, DT * H], BF16)
    gt_sb = p.sbuf(sfx + "_gt", [128, DT * H], BF16)
    y_sb = xn_sb
    stA = p.sbuf(sfx + "_stA", [128, DT * CB], BF16)
    stB = p.sbuf(sfx + "_stB", [128, DT * CB], BF16)
    tt0 = p.sbuf(sfx + "_tt0", [128, 2 * 512], BF16)
    tu0 = p.sbuf(sfx + "_tu0", [128, 2 * 512], F32)
    t0f = p.sbuf(sfx + "_t0f", [128, 2 * 512], F32)
    ob = p.sbuf(sfx + "_ob", [128, 2 * 512], F32)
    gx_g = p.sbuf(sfx + "_gxg", [128, 2 * H], BF16)
    gx_x = p.sbuf(sfx + "_gxx", [128, 2 * H], F32)
    bcols = p.sbuf(sfx + "_bc", [128, 5 * 16], F32)

    pool = Pool(p, sfx, max_nk=16, cbw=512, dual=True)
    ld = p.sem(sfx + "_ld")
    bld = p.sem(sfx + "_bld")
    vv = p.sem(sfx + "_vv")
    xg = p.sem(sfx + "_xg")
    x1v = p.sem(sfx + "_x1v")
    st = [p.sem(sfx + "_st0"), p.sem(sfx + "_st1")]

    for i, apn in enumerate(["b_in", "b_dec", "b_gate", "b_out", "dec_bias"]):
        p.inc(g_.dma_start(out=bcols[:, i * 16:(i + 1) * 16], in_=bias_aps[apn]), bld, 16)
    p.wait(v, bld)
    i0 = v.tensor_add(bcols[:, 16:32], bcols[:, 16:32], bcols[:, 64:80])
    p.inc(i0, vv)
    vv_bias = p.val(vv)
    p.wait(s_, vv, vv_bias)

    def bc(i, m):
        return bcols[:, i * 16 + m:i * 16 + m + 1]

    for hb in range(2):
        hs = slice(hb * H, (hb + 1) * H)
        if hb > 0:
            p.wait(sy, st[0])
            p.wait(sy, st[1])
        for j in range(DT):
            p.inc(sy.dma_start(out=xn_sb[:, j * H:(j + 1) * H],
                               in_=xn_ap[j * 128:(j + 1) * 128, hs]), ld, 16)
        ld_xn = p.val(ld)
        if hb > 0:
            p.wait(s_, vv)

        def evac1(mi, cb, pa, pb):
            n = pool.n_group - 1
            par = n % 2
            b0 = p.val(pool.bk)
            p.wait(s_, pool.mm)
            if n >= 2:
                p.wait(s_, vv, p.val(vv) - 1)
            i1 = s_.activation(tt0[:, par * 512:(par + 1) * 512], pa, AF.Silu, bias=bc(0, mi))
            p.inc(i1, pool.bk)
            i2 = s_.activation(d_sb[:, mi * H + cb * 512: mi * H + (cb + 1) * 512], pb,
                               AF.Sigmoid, bias=bc(1, mi))
            p.inc(i2, pool.bk)
            p.wait(v, pool.bk)
            v.tensor_scalar(tu0[:, par * 512:(par + 1) * 512],
                            d_sb[:, mi * H + cb * 512: mi * H + (cb + 1) * 512],
                            -1.0, 1.0, ALU.mult, ALU.add)
            v.drain()
            i3 = v.tensor_mul(u_sb[:, mi * H + cb * 512: mi * H + (cb + 1) * 512],
                              tu0[:, par * 512:(par + 1) * 512],
                              tt0[:, par * 512:(par + 1) * 512])
            p.inc(i3, vv)
            return p.val(pool.bk) - b0

        mm_pass(p, pool, src_sb=xn_sb, src_cols=H, n_k=16, n_m=16,
                Wimg=imgs["win"], W2img=imgs["wdec"], evac=evac1, src_wait=(ld, ld_xn))

        def evac2(mi, cb, pa, pb):
            b0 = p.val(pool.bk)
            p.wait(s_, pool.mm)
            i1 = s_.activation(gt_sb[:, mi * H + cb * 512: mi * H + (cb + 1) * 512], pa,
                               AF.Sigmoid, bias=bc(2, mi))
            p.inc(i1, pool.bk)
            return p.val(pool.bk) - b0

        mm_pass(p, pool, src_sb=xn_sb, src_cols=H, n_k=16, n_m=16,
                Wimg=imgs["wgate"], evac=evac2, src_wait=(ld, ld_xn))

        p.wait(v, pool.bk)
        p.wait(v, vv)
        for cb in range(H // CB):
            gcb = hb * (H // CB) + cb
            cA, cB = (stA, stB) if gcb % 2 == 0 else (stB, stA)
            v.drain()
            for j in range(DT):
                dsl = d_sb[:, j * H + cb * CB: j * H + (cb + 1) * CB]
                usl = u_sb[:, j * H + cb * CB: j * H + (cb + 1) * CB]
                init = 0.0 if gcb == 0 else cB[:, j * CB + CB - 1: j * CB + CB]
                v.tensor_tensor_scan(cA[:, j * CB:(j + 1) * CB], dsl, usl, init,
                                     ALU.mult, ALU.add)
            v.drain()
            for j in range(DT):
                i1 = v.tensor_mul(y_sb[:, j * H + cb * CB: j * H + (cb + 1) * CB],
                                  gt_sb[:, j * H + cb * CB: j * H + (cb + 1) * CB],
                                  cA[:, j * CB:(j + 1) * CB])
            p.inc(i1, vv)
        vv_scan = p.val(vv)

        def on_m3(mi):
            par = mi % 2
            if mi >= 2:
                p.wait(sy, x1v, p.val(x1v) - 2)
            p.inc(sy.dma_start(out=gx_g[:, par * H:(par + 1) * H],
                               in_=g_ap[mi * 128:(mi + 1) * 128, hs]), xg, 16)
            p.inc(sy.dma_start(out=gx_x[:, par * H:(par + 1) * H],
                               in_=xT_ap[mi * 128:(mi + 1) * 128, hs]), xg, 16)

        def evac3(mi, cb, pa, pb, hb=hb):
            n = pool.n_group - 1
            par2 = n % 2
            parm = mi % 2
            b0 = p.val(pool.bk)
            p.wait(v, pool.mm)
            p.wait(v, xg)
            if p.val(st[par2]) >= 16:
                p.wait(v, st[par2], p.val(st[par2]))
            i1 = v.scalar_tensor_tensor(
                t0f[:, par2 * 512:(par2 + 1) * 512], pa, bc(3, mi),
                gx_g[:, parm * H + cb * 512: parm * H + (cb + 1) * 512],
                ALU.add, ALU.mult)
            p.inc(i1, pool.bk)
            v.drain()
            i2 = v.tensor_add(ob[:, par2 * 512:(par2 + 1) * 512],
                              t0f[:, par2 * 512:(par2 + 1) * 512],
                              gx_x[:, parm * H + cb * 512: parm * H + (cb + 1) * 512])
            p.inc(i2, x1v)
            p.wait(sy, x1v)
            p.inc(sy.dma_start(out=x1_ap[mi * 128:(mi + 1) * 128,
                                         hb * H + cb * 512: hb * H + (cb + 1) * 512],
                               in_=ob[:, par2 * 512:(par2 + 1) * 512]), st[par2], 16)
            return p.val(pool.bk) - b0

        if io.get("dbg") is not None:
            dbg = io["dbg"]
            p.wait(sy, vv)
            p.inc(sy.dma_start(out=dbg["bc"], in_=bcols[:, :]), st[0], 16)
            for j in range(DT):
                p.inc(sy.dma_start(out=dbg["d"][j * 128:(j + 1) * 128, hs], in_=d_sb[:, j * H:(j + 1) * H]), st[0], 16)
                p.inc(sy.dma_start(out=dbg["u"][j * 128:(j + 1) * 128, hs], in_=u_sb[:, j * H:(j + 1) * H]), st[0], 16)
                p.inc(sy.dma_start(out=dbg["gt"][j * 128:(j + 1) * 128, hs], in_=gt_sb[:, j * H:(j + 1) * H]), st[0], 16)
                p.inc(sy.dma_start(out=dbg["y"][j * 128:(j + 1) * 128, hs], in_=y_sb[:, j * H:(j + 1) * H]), st[0], 16)
            p.wait(sy, st[0])
        mm_pass(p, pool, src_sb=y_sb, src_cols=H, n_k=16, n_m=16,
                Wimg=imgs["wout"], evac=evac3, on_m=on_m3, src_wait=(vv, vv_scan))
    p.wait(sy, st[0])
    p.wait(sy, st[1])


def build_s2(dbg=False):
    p = P()
    io = {}
    if dbg:
        io["dbg"] = {k: p.dram("dbg_" + k, [2048, S], BF16, "ExternalOutput")
                     for k in ["d", "u", "gt", "y"]}
        io["dbg"]["bc"] = p.dram("dbg_bc", [128, 80], F32, "ExternalOutput")
    xn = p.dram("xn_bf", [2048, S], BF16, "ExternalInput")
    xT = p.dram("xT", [2048, S], F32, "ExternalInput")
    gb = p.dram("g_bf", [2048, S], BF16, "ExternalInput")
    x1 = p.dram("x1", [2048, S], F32, "ExternalOutput")
    imgs = {k: p.dram(k + "_img", [2048, 2048], F32, "ExternalInput")
            for k in ["win", "wdec", "wgate", "wout"]}
    bias_aps = {k: p.dram(k + "_col", [128, 16], F32, "ExternalInput")
                for k in ["b_in", "b_dec", "b_gate", "b_out", "dec_bias"]}
    load_consts(p, io)
    emit_l0_attn(p, io, "s2", xn, xT, gb, x1, imgs, bias_aps)
    p.close()
    return p




def emit_moe(p, io, sfx, xn_ap, w_ap, wg_ap, wu_ap, wd_ap, g_ap):
    nc = p.nc
    g_, v, t, s_, sy = nc.gpsimd, nc.vector, nc.tensor, nc.scalar, nc.sync
    H = 1024
    DT = 16

    xn_sb = p.sbuf(sfx + "_xn", [128, DT * H], BF16)
    hp_sb = p.sbuf(sfx + "_hp", [128, DT * H], BF16)
    acc_sb = p.sbuf(sfx + "_acc", [128, DT * H], BF16)
    wb_sb = p.sbuf(sfx + "_wbc", [128, H], BF16)
    wrow = p.sbuf(sfx + "_wrow", [1, H], F32)
    tt = p.sbuf(sfx + "_tt", [128, 2 * 512], BF16)
    gbuf = p.sbuf(sfx + "_gb", [128, 2 * H], BF16)
    ps_w = p.psum([128, H])

    pool = Pool(p, sfx, max_nk=16, cbw=512, dual=True)
    ld = p.sem(sfx + "_ld")
    wld = p.sem(sfx + "_wld")
    wmm = p.sem(sfx + "_wmm")
    wbv = p.sem(sfx + "_wbv")
    sv = p.sem(sfx + "_sv")
    gv = p.sem(sfx + "_gv")
    st = p.sem(sfx + "_st")

    p.wait(g_, io["c_ld"])
    ones_f32 = io["ones_f32_sb"]

    for hb in range(2):
        hs = slice(hb * H, (hb + 1) * H)
        if hb > 0:
            p.wait(sy, st)
        for j in range(DT):
            p.inc(sy.dma_start(out=xn_sb[:, j * H:(j + 1) * H],
                               in_=xn_ap[j * 128:(j + 1) * 128, hs]), ld, 16)
        ld_xn = p.val(ld)
        if hb > 0:
            p.wait(v, gv)

        for e in range(8):
            if e > 0 or hb > 0:
                p.wait(g_, wmm)
            p.inc(g_.dma_start(out=wrow[0:1, :], in_=w_ap[e:e + 1, hs]), wld, 16)
            p.wait(t, wld)
            if e > 0 or hb > 0:
                p.wait(t, wbv)
            t.matmul(ps_w[:, 0:512], ones_f32[0:1, :], wrow[0:1, 0:512], start=True, stop=True)
            i1 = t.matmul(ps_w[:, 512:1024], ones_f32[0:1, :], wrow[0:1, 512:1024], start=True, stop=True)
            p.inc(i1, wmm)
            p.wait(v, wmm)
            p.inc(v.tensor_copy(wb_sb[:, :], ps_w[:, :]), wbv)

            def evacA(mi, cb, pa, pb, e=e):
                n = pool.n_group - 1
                par = n % 2
                p.wait(s_, pool.mm)
                if p.val(pool.bk) >= 2:
                    p.wait(s_, pool.bk, p.val(pool.bk) - 1)
                i1 = s_.activation(tt[:, par * 512:(par + 1) * 512], pa, AF.Silu)
                p.inc(i1, sv)
                p.wait(v, sv)
                p.wait(v, wbv)
                dsl = hp_sb[:, mi * H + cb * 512: mi * H + (cb + 1) * 512]
                v.tensor_mul(dsl, tt[:, par * 512:(par + 1) * 512], pb)
                v.drain()
                i2 = v.tensor_mul(dsl, dsl,
                                  wb_sb[:, cb * 512:(cb + 1) * 512])
                p.inc(i2, pool.bk)
                return 1

            mm_pass(p, pool, src_sb=xn_sb, src_cols=H, n_k=16, n_m=16,
                    Wimg=wg_ap[e * 2048:(e + 1) * 2048, :],
                    W2img=wu_ap[e * 2048:(e + 1) * 2048, :],
                    evac=evacA, src_wait=(ld, ld_xn))
            hv_tgt = p.val(pool.bk)

            def evacB(mi, cb, pa, pb, e=e):
                p.wait(v, pool.mm)
                dsl = acc_sb[:, mi * H + cb * 512: mi * H + (cb + 1) * 512]
                if e == 0:
                    i1 = v.tensor_copy(dsl, pa)
                else:
                    i1 = v.tensor_add(dsl, dsl, pa)
                p.inc(i1, pool.bk)
                return 1

            mm_pass(p, pool, src_sb=hp_sb, src_cols=H, n_k=16, n_m=16,
                    Wimg=wd_ap[e * 2048:(e + 1) * 2048, :],
                    evac=evacB, src_wait=(pool.bk, hv_tgt))

        p.wait(s_, pool.bk)
        for j in range(DT):
            parj = j % 2
            if j >= 2:
                p.wait(s_, st, p.val(st))
            i1 = s_.activation(gbuf[:, parj * H:(parj + 1) * H],
                               acc_sb[:, j * H:(j + 1) * H], AF.Silu)
            p.inc(i1, gv)
            p.wait(sy, gv)
            p.inc(sy.dma_start(out=g_ap[j * 128:(j + 1) * 128, hs],
                               in_=gbuf[:, parj * H:(parj + 1) * H]), st, 16)
    p.wait(sy, st)


def build_s1():
    p = P()
    io = {}
    xn = p.dram("xn_bf", [2048, S], BF16, "ExternalInput")
    wf = p.dram("w_fm", [8, S], F32, "ExternalInput")
    wg = p.dram("wg_img", [8 * 2048, 2048], F32, "ExternalInput")
    wu = p.dram("wu_img", [8 * 2048, 2048], F32, "ExternalInput")
    wd = p.dram("wd_img", [8 * 2048, 2048], F32, "ExternalInput")
    gb = p.dram("g_bf", [2048, S], BF16, "ExternalOutput")
    load_consts(p, io)
    emit_moe(p, io, "s1", xn, wf, wg, wu, wd, gb)
    p.close()
    return p



def emit_ffn(p, io, sfx, xn_ap, w0_ap, w1_ap, b0_ap, b1_ap, g2_ap):
    nc = p.nc
    g_, v, t, s_, sy = nc.gpsimd, nc.vector, nc.tensor, nc.scalar, nc.sync
    Q = 1024
    DT = 16

    xn_sb = p.sbuf(sfx + "_xn", [128, DT * Q], BF16)
    dff_sb = p.sbuf(sfx + "_dff", [128, 64 * Q], BF16)
    g2buf = p.sbuf(sfx + "_g2b", [128, 2 * 512], BF16)
    bcols = p.sbuf(sfx + "_bc", [128, 64 + 16], F32)

    pool = Pool(p, sfx, max_nk=64, cbw=512, dual=False, depth=2)
    ld = p.sem(sfx + "_ld")
    bld = p.sem(sfx + "_bld")
    st = p.sem(sfx + "_st")
    stp = [p.sem(sfx + "_stp0"), p.sem(sfx + "_stp1")]

    p.wait(g_, io["c_ld"])
    p.inc(g_.dma_start(out=bcols[:, 0:64], in_=b0_ap), bld, 16)
    p.inc(g_.dma_start(out=bcols[:, 64:80], in_=b1_ap), bld, 16)
    p.wait(s_, bld)

    for q in range(S // Q):
        qs = slice(q * Q, (q + 1) * Q)
        if q > 0:
            p.wait(sy, stp[0])
            p.wait(sy, stp[1])
        for j in range(DT):
            p.inc(sy.dma_start(out=xn_sb[:, j * Q:(j + 1) * Q],
                               in_=xn_ap[j * 128:(j + 1) * 128, qs]), ld, 16)
        ld_xn = p.val(ld)

        def evac0(mi, cb, pa, pb):
            p.wait(s_, pool.mm)
            i1 = s_.activation(dff_sb[:, mi * Q + cb * 512: mi * Q + (cb + 1) * 512],
                               pa, AF.Gelu, bias=bcols[:, mi:mi + 1])
            p.inc(i1, pool.bk)
            return 1

        mm_pass(p, pool, src_sb=xn_sb, src_cols=Q, n_k=16, n_m=64,
                Wimg=w0_ap, evac=evac0, src_wait=(ld, ld_xn))

        def evac1(mi, cb, pa, pb, q=q):
            n = pool.n_group - 1
            par = n % 2
            p.wait(s_, pool.mm)
            if p.val(stp[par]) >= 16:
                p.wait(s_, stp[par], p.val(stp[par]))
            i1 = s_.activation(g2buf[:, par * 512:(par + 1) * 512], pa, AF.Silu,
                               bias=bcols[:, 64 + mi:64 + mi + 1])
            p.inc(i1, pool.bk)
            p.wait(sy, pool.bk)
            i2 = sy.dma_start(out=g2_ap[mi * 128:(mi + 1) * 128,
                                        q * Q + cb * 512: q * Q + (cb + 1) * 512],
                              in_=g2buf[:, par * 512:(par + 1) * 512])
            p.inc(i2, stp[par], 16)
            return 1

        mm_pass(p, pool, src_sb=dff_sb, src_cols=Q, n_k=64, n_m=16,
                Wimg=w1_ap, evac=evac1, src_wait=(pool.bk, p.val(pool.bk)))
    p.wait(sy, stp[0])
    p.wait(sy, stp[1])


def build_s3():
    p = P()
    io = {}
    x1 = p.dram("x1", [2048, S], F32, "ExternalInput")
    xn2 = p.dram("xn2_bf", [2048, S], BF16, "ExternalOutput")
    load_consts(p, io)
    emit_ln_router(p, io, "s3", x1, xn2, router=False)
    p.close()
    return p


def build_s4():
    p = P()
    io = {}
    xn2 = p.dram("xn2_bf", [2048, S], BF16, "ExternalInput")
    w0 = p.dram("w0_img", [8192, 2048], F32, "ExternalInput")
    w1 = p.dram("w1_img", [2048, 8192], F32, "ExternalInput")
    b0 = p.dram("b0_col", [128, 64], F32, "ExternalInput")
    b1 = p.dram("b1_col", [128, 16], F32, "ExternalInput")
    g2 = p.dram("g2_bf", [2048, S], BF16, "ExternalOutput")
    load_consts(p, io)
    emit_ffn(p, io, "s4", xn2, w0, w1, b0, b1, g2)
    p.close()
    return p




def emit_zxbcdt(p, io, sfx, xn_ap, win_ap, taps_ap, bconv_ap, dtb_ap, alog_ap,
                z_ap, xbc_ap, a_ap):
    nc = p.nc
    g_, v, t, s_, sy = nc.gpsimd, nc.vector, nc.tensor, nc.scalar, nc.sync
    H = 1024
    DT = 16

    xn_sb = p.sbuf(sfx + "_xn", [128, DT * H], BF16)
    zbuf = p.sbuf(sfx + "_zb", [128, 2 * 512], BF16)
    ext = p.sbuf(sfx + "_ext", [128, H + 3], F32)
    cbuf = p.sbuf(sfx + "_cb", [128, 4 * 512], F32)
    xbcb = p.sbuf(sfx + "_xbcb", [128, 2 * 512], BF16)
    hist = p.sbuf(sfx + "_hist", [128, 34 * 3], F32)
    taps = p.sbuf(sfx + "_taps", [128, 34 * 4], F32)
    bconv = p.sbuf(sfx + "_bcv", [128, 34], F32)
    dtb = p.sbuf(sfx + "_dtb", [64, 1], F32)
    alog = p.sbuf(sfx + "_alog", [64, 1], F32)
    nega = p.sbuf(sfx + "_nega", [64, 2], F32)
    dts = p.sbuf(sfx + "_dts", [64, 6 * 512], F32)
    abuf = p.sbuf(sfx + "_ab", [64, 2 * 512], F32)

    pool = Pool(p, sfx, max_nk=16, cbw=512, dual=False)
    ld = p.sem(sfx + "_ld")
    bld = p.sem(sfx + "_bld")
    st = p.sem(sfx + "_st")
    cv = p.sem(sfx + "_cv")
    sc2 = p.sem(sfx + "_sc2")

    p.wait(g_, io["c_ld"])
    for ap, sb in [(taps_ap, taps), (bconv_ap, bconv), (dtb_ap, dtb), (alog_ap, alog)]:
        p.inc(g_.dma_start(out=sb[:, :] if sb is not dtb and sb is not alog else sb[0:64, :], in_=ap), bld, 16)
    p.wait(s_, bld)
    i0 = s_.activation(nega[0:64, 0:1], alog[0:64, 0:1], AF.Exp)
    p.inc(i0, sc2)
    p.wait(v, sc2)
    i1 = v.tensor_scalar_mul(nega[0:64, 1:2], nega[0:64, 0:1], -1.0)
    p.inc(i1, cv)
    p.wait(v, bld)
    v.memset(hist[:, :], 0.0)
    i2 = v.memset(ext[:, :], 0.0)
    p.inc(i2, cv)

    for hb in range(2):
        hs = slice(hb * H, (hb + 1) * H)
        if hb > 0:
            p.wait(sy, st)
        for j in range(DT):
            p.inc(sy.dma_start(out=xn_sb[:, j * H:(j + 1) * H],
                               in_=xn_ap[j * 128:(j + 1) * 128, hs]), ld, 16)
        ld_xn = p.val(ld)

        def evac(mi, cb, pa, pb, hb=hb, hs=hs):
            n = pool.n_group - 1
            par = n % 2
            if mi < 32:
                p.wait(s_, pool.mm)
                p.wait(s_, st, p.val(st))
                i1 = s_.activation(zbuf[:, par * 512:(par + 1) * 512], pa, AF.Silu)
                p.inc(i1, pool.bk)
                p.wait(sy, pool.bk)
                p.inc(sy.dma_start(out=z_ap[mi * 128:(mi + 1) * 128,
                                            hb * H + cb * 512: hb * H + (cb + 1) * 512],
                                   in_=zbuf[:, par * 512:(par + 1) * 512]), st, 16)
                return 1
            if mi < 66:
                ci = mi - 32
                p.wait(s_, pool.mm)
                if cb == 0:
                    p.wait(s_, cv)
                i1 = s_.activation(ext[:, 3 + cb * 512: 3 + (cb + 1) * 512], pa, AF.Identity)
                p.inc(i1, pool.bk)
                p.wait(v, pool.bk)
                c0 = cb * 512
                t1 = cbuf[:, (par * 2) * 512:(par * 2 + 1) * 512]
                t2 = cbuf[:, (par * 2 + 1) * 512:(par * 2 + 2) * 512]
                if cb == 0 and hb > 0:
                    v.tensor_copy(ext[:, 0:3], hist[:, ci * 3:(ci + 1) * 3])
                    v.drain()
                v.tensor_scalar_mul(t1, ext[:, c0:c0 + 512], taps[:, ci * 4: ci * 4 + 1])
                v.drain()
                v.scalar_tensor_tensor(t2, ext[:, c0 + 1:c0 + 513], taps[:, ci * 4 + 1: ci * 4 + 2],
                                       t1, ALU.mult, ALU.add)
                v.drain()
                v.scalar_tensor_tensor(t1, ext[:, c0 + 2:c0 + 514], taps[:, ci * 4 + 2: ci * 4 + 3],
                                       t2, ALU.mult, ALU.add)
                v.drain()
                i2 = v.scalar_tensor_tensor(t2, ext[:, c0 + 3:c0 + 515], taps[:, ci * 4 + 3: ci * 4 + 4],
                                            t1, ALU.mult, ALU.add)
                if cb == 1:
                    v.drain()
                    i2 = v.tensor_copy(hist[:, ci * 3:(ci + 1) * 3], ext[:, H:H + 3])
                p.inc(i2, cv)
                p.wait(s_, cv)
                p.wait(s_, st, p.val(st))
                i3 = s_.activation(xbcb[:, par * 512:(par + 1) * 512], t2, AF.Silu,
                                   bias=bconv[:, ci:ci + 1])
                p.inc(i3, pool.bk)
                p.wait(sy, pool.bk)
                p.inc(sy.dma_start(out=xbc_ap[ci * 128:(ci + 1) * 128,
                                              hb * H + cb * 512: hb * H + (cb + 1) * 512],
                                   in_=xbcb[:, par * 512:(par + 1) * 512]), st, 16)
                return 2
            p.wait(s_, pool.mm)
            p.wait(s_, cv, p.val(cv))
            xr = dts[0:64, 0:512]
            a1 = dts[0:64, 512:1024]
            i1 = s_.activation(xr, pa[0:64, :], AF.Identity, bias=dtb[0:64, 0:1])
            s_.drain()
            s_.activation(a1, xr, AF.Abs)
            s_.drain()
            s_.activation(dts[0:64, 1024:1536], a1, AF.Exp, scale=-1.0)
            s_.drain()
            i2 = s_.activation(dts[0:64, 1536:2048], dts[0:64, 1024:1536], AF.Ln, bias=1.0)
            p.inc(i2, pool.bk)
            p.wait(v, pool.bk)
            v.tensor_scalar_max(dts[0:64, 2048:2560], xr, 0.0)
            v.drain()
            v.tensor_add(dts[0:64, 2560:3072], dts[0:64, 2048:2560], dts[0:64, 1536:2048])
            v.drain()
            pard = pool.n_group % 2
            p.wait(v, st, p.val(st))
            i3 = v.tensor_scalar_mul(abuf[0:64, pard * 512:(pard + 1) * 512],
                                     dts[0:64, 2560:3072], nega[0:64, 1:2])
            p.inc(i3, cv)
            p.wait(sy, cv)
            p.inc(sy.dma_start(out=a_ap[0:64, hb * H + cb * 512: hb * H + (cb + 1) * 512],
                               in_=abuf[0:64, pard * 512:(pard + 1) * 512]), st, 16)
            return 1

        mm_pass(p, pool, src_sb=xn_sb, src_cols=H, n_k=16, n_m=67,
                Wimg=win_ap, evac=evac, src_wait=(ld, ld_xn))
    p.wait(sy, st)


def build_s5():
    p = P()
    io = {}
    xn2 = p.dram("xn2_bf", [2048, S], BF16, "ExternalInput")
    win = p.dram("win_img", [67 * 128, 2048], F32, "ExternalInput")
    taps = p.dram("taps", [128, 34 * 4], F32, "ExternalInput")
    bconv = p.dram("bconv", [128, 34], F32, "ExternalInput")
    dtb = p.dram("dtb", [64, 1], F32, "ExternalInput")
    alog = p.dram("alog", [64, 1], F32, "ExternalInput")
    z = p.dram("z_silu", [4096, S], BF16, "ExternalOutput")
    xbc = p.dram("xBC_bf", [34 * 128, S], BF16, "ExternalOutput")
    a = p.dram("A_arr", [64, S], F32, "ExternalOutput")
    load_consts(p, io)
    emit_zxbcdt(p, io, "s5", xn2, win, taps, bconv, dtb, alog, z, xbc, a)
    p.close()
    return p




def emit_ssd(p, io, sfx, xbc_ap, z_ap, a_ap, dp_ap, wn_ap, yn_ap):
    nc = p.nc
    g_, v, t, s_, sy = nc.gpsimd, nc.vector, nc.tensor, nc.scalar, nc.sync
    NCH = 16

    xbc_sb2 = p.sbuf(sfx + "_xbc", [128, 2 * 34 * 128], BF16)
    z_sb2 = p.sbuf(sfx + "_z", [128, 2 * 32 * 128], BF16)
    a_sb2 = p.sbuf(sfx + "_a", [64, 2 * 128], F32)
    cs = p.sbuf(sfx + "_cs", [64, 128], F32)
    z64 = p.sbuf(sfx + "_z64", [64, 128], F32)
    a64 = p.sbuf(sfx + "_a64", [64, 4 * 128], BF16)
    cm = p.sbuf(sfx + "_cm", [64, 128], F32)
    gsel = p.sbuf(sfx + "_gsel", [64, 4096], BF16)
    dec_T = p.sbuf(sfx + "_decT", [128, 64], BF16)
    stage1 = p.sbuf(sfx + "_stg1", [4, 8192], BF16)
    stage2 = p.sbuf(sfx + "_stg2", [4, 8192], BF16)
    X_T = p.sbuf(sfx + "_XT", [128, 4096], BF16)
    Xw_T = p.sbuf(sfx + "_XwT", [128, 4096], BF16)
    B_T = p.sbuf(sfx + "_BT", [128, 128], BF16)
    G_m = p.sbuf(sfx + "_Gm", [128, 128], BF16)
    L_sb = p.sbuf(sfx + "_L", [128, 2 * 1024], BF16)
    M_sb = p.sbuf(sfx + "_M", [128, 8192], BF16)
    sdo = p.sbuf(sfx + "_sdo", [128, 4096], BF16)
    y3 = p.sbuf(sfx + "_y3", [128, 4096], BF16)
    sq = p.sbuf(sfx + "_sq", [128, 4096], BF16)
    F_sb = sq
    ynb = p.sbuf(sfx + "_ynb", [128, 4096], BF16)
    tf = p.sbuf(sfx + "_tf", [128, 2 * 512], F32)
    run = p.sbuf(sfx + "_run", [128, 4096], F32)
    run_bf = p.sbuf(sfx + "_runbf", [128, 4096], BF16)
    dg = p.sbuf(sfx + "_dg", [128, 32 * 128], BF16)
    dp_sb = p.sbuf(sfx + "_dp", [128, 32], F32)
    wn_sb = p.sbuf(sfx + "_wn", [128, 32], F32)
    rowbuf = p.sbuf(sfx + "_row", [1, 4 * 128], F32)
    inv_b = p.sbuf(sfx + "_invb", [128, 128], BF16)
    zeros64 = p.sbuf(sfx + "_z0", [64, 128], F32)

    ps1 = p.psum([128, 1024])
    ps2 = p.psum([128, 1024])
    ps3 = p.psum([128, 512])
    ps4 = p.psum([128, 512])
    psT = p.psum([128, 1024], BF16)
    psR = p.psum([128, 128])

    ld = [p.sem(sfx + "_ld0"), p.sem(sfx + "_ld1")]
    cdone = p.sem(sfx + "_cdone")
    bld = p.sem(sfx + "_bld")
    tv = p.sem(sfx + "_tv")
    vv = p.sem(sfx + "_vv")
    sc = p.sem(sfx + "_sc")
    gp = p.sem(sfx + "_gp")
    sg = p.sem(sfx + "_sg")
    st = p.sem(sfx + "_st")

    ones_bf = io["ones_bf_sb"]
    ones_f32 = io["ones_f32_sb"]
    ident_bf = io["ident_bf_sb"]
    ident_f32 = io["ident_f32_sb"]
    triu01 = io["triu01_sb"]
    sel = p.sbuf(sfx + "_sel", [64, 32 * 128], BF16)
    sel64 = p.sbuf(sfx + "_sel64", [64, 4096], BF16)

    p.wait(g_, io["c_ld"])
    p.inc(g_.dma_start(out=sel[:, :], in_=io["sel_ap"]), bld, 16)
    p.inc(g_.dma_start(out=sel64[:, :], in_=io["sel64_ap"]), bld, 16)
    p.inc(g_.dma_start(out=dp_sb[:, :], in_=dp_ap), bld, 16)
    p.inc(g_.dma_start(out=wn_sb[:, :], in_=wn_ap), bld, 16)
    p.wait(v, bld)
    p.wait(t, bld)
    p.wait(v, io["c_ld"])
    v.memset(run[:, :], 0.0)
    v.memset(zeros64[:, :], 0.0)
    v.memset(stage1[0:4, :], 1.0)
    v.memset(stage2[0:4, :], 1.0)
    v.drain()
    for j in range(32):
        i1 = v.tensor_scalar_mul(dg[:, j * 128:(j + 1) * 128], ident_bf[:, :],
                                 dp_sb[:, j:j + 1])
    p.inc(i1, vv)

    def issue_loads(c):
        par = c % 2
        csl = slice(c * 128, (c + 1) * 128)
        if c >= 2:
            p.wait(sy, cdone, c - 1)
        for j0, jn in [(0, 17), (17, 17)]:
            p.inc(sy.dma_start(
                out=xbc_sb2[:, (par * 34 + j0) * 128:(par * 34 + j0 + jn) * 128]
                    .rearrange("p (j c) -> p j c", j=jn),
                in_=xbc_ap[j0 * 128:(j0 + jn) * 128, csl]
                    .rearrange("(j p) c -> p j c", p=128)), ld[par], 16)
        for j0 in (0, 16):
            p.inc(sy.dma_start(
                out=z_sb2[:, (par * 32 + j0) * 128:(par * 32 + j0 + 16) * 128]
                    .rearrange("p (j c) -> p j c", j=16),
                in_=z_ap[j0 * 128:(j0 + 16) * 128, csl]
                    .rearrange("(j p) c -> p j c", p=128)), ld[par], 16)
        p.inc(sy.dma_start(out=a_sb2[0:64, par * 128:(par + 1) * 128],
                           in_=a_ap[0:64, csl]), ld[par], 16)

    issue_loads(0)
    xw_vv = []
    psr_sc = []
    for c in range(NCH):
        parc_in = c % 2
        if c + 1 < NCH:
            issue_loads(c + 1)
        xbc_sb = xbc_sb2[:, parc_in * 34 * 128:(parc_in + 1) * 34 * 128]
        z_sb = z_sb2[:, parc_in * 32 * 128:(parc_in + 1) * 32 * 128]
        a_sb = a_sb2[:, parc_in * 128:(parc_in + 1) * 128]
        csl = slice(c * 128, (c + 1) * 128)
        ld_c = p.val(ld[parc_in])

        p.wait(v, ld[parc_in], ld_c)
        v.drain()
        v.tensor_tensor_scan(cs[0:64, :], a_sb[0:64, :], zeros64[0:64, :], 0.0,
                             ALU.add, ALU.add)
        v.drain()
        v.tensor_copy(a64[0:64, 0:128], cs[0:64, :])
        v.drain()
        v.tensor_sub(a64[0:64, 128:256], cs[0:64, :], a64[0:64, 0:128])
        v.drain()
        v.tensor_scalar_mul(a64[0:64, 256:384], a64[0:64, 0:128], -1.0)
        v.tensor_scalar_mul(a64[0:64, 384:512], a64[0:64, 128:256], -1.0)
        v.drain()
        i1 = v.tensor_single_scalar(cm[0:64, :], cs[0:64, :], cs[0:64, 127:128],
                                    ALU.subtract)
        p.inc(i1, vv)
        p.wait(s_, vv)
        i2 = s_.activation(z64[0:64, 0:1], cs[0:64, 127:128], AF.Exp)
        p.inc(i2, sc)
        p.wait(v, sc)
        i3 = v.tensor_scalar_mul(gsel[0:64, :], sel64[0:64, :], z64[0:64, 0:1])
        p.inc(i3, vv)
        p.wait(g_, vv)
        p.inc(g_.dma_start(out=stage2[2:3, :], in_=a64[0:64, 0:128]), sg, 16)
        p.inc(g_.dma_start(out=stage2[3:4, :], in_=a64[0:64, 128:256]), sg, 16)
        p.inc(g_.dma_start(out=stage1[0:1, :], in_=a64[0:64, 256:384]), sg, 16)
        p.inc(g_.dma_start(out=stage1[1:2, :], in_=a64[0:64, 384:512]), sg, 16)

        p.wait(t, ld[parc_in], ld_c)
        if c > 0:
            p.wait(t, vv, xw_vv[-1])
        for r in range(4):
            for i in range(8):
                i6 = t.transpose(psT[:, i * 128:(i + 1) * 128],
                                 xbc_sb[:, (r * 8 + i) * 128:(r * 8 + i + 1) * 128],
                                 ident_bf[:, :])
            p.inc(i6, tv)
            p.wait(v, tv)
            i7 = v.tensor_copy(X_T[:, r * 1024:(r + 1) * 1024], psT[:, :])
            p.inc(i7, vv)
            p.wait(t, vv)
        i8 = t.transpose(psT[:, 0:128], xbc_sb[:, 32 * 128:33 * 128], ident_bf[:, :])
        p.inc(i8, tv)
        p.wait(v, tv)
        i9 = v.tensor_copy(B_T[:, :], psT[:, 0:128])
        p.inc(i9, vv)

        p.wait(t, vv)
        if c > 0:
            p.wait(t, sc, psr_sc[-1])
        i4 = t.transpose(psR[:, 0:64], cm[0:64, :], ident_f32[0:64, 0:64])
        p.inc(i4, tv)
        p.wait(s_, tv)
        i5 = s_.activation(dec_T[:, 0:64], psR[:, 0:64], AF.Exp, scale=-1.0)
        p.inc(i5, sc)
        dec_sc = p.val(sc)

        p.wait(v, sc, dec_sc)
        v.drain()
        i10 = v.tensor_tensor(Xw_T[:, :].rearrange("p (h q) -> p h q", h=64),
                              X_T[:, :].rearrange("p (h q) -> p h q", h=64),
                              dec_T[:, 0:64].unsqueeze(2).broadcast_to([128, 64, 64]),
                              ALU.mult)
        p.inc(i10, vv)
        xw_vv.append(p.val(vv))

        p.wait(t, sc, dec_sc)
        i11 = t.matmul(psR[:, :], xbc_sb[:, 32 * 128:33 * 128],
                       xbc_sb[:, 33 * 128:34 * 128], start=True, stop=True)
        p.inc(i11, tv)
        p.wait(v, tv)
        i12 = v.tensor_tensor(G_m[:, :], psR[:, :], triu01[:, :], ALU.mult)
        p.inc(i12, vv)
        psr_sc.append(p.val(sc))

        p.wait(t, sg)
        p.wait(t, vv)
        p.wait(t, sc)
        sdo_sc = []
        for r in range(4):
            pp = (ps1, ps2)[r % 2]
            if r >= 2:
                p.wait(t, sc, sdo_sc[r - 2])
            for i in range(8):
                j = r * 8 + i
                t.matmul(pp[:, i * 128:(i + 1) * 128], sel[0:64, j * 128:(j + 1) * 128],
                         a64[0:64, 0:128], start=True, stop=False)
                i13 = t.matmul(pp[:, i * 128:(i + 1) * 128], sel[0:64, j * 128:(j + 1) * 128],
                               a64[0:64, 128:256], start=False, stop=True)
            p.inc(i13, tv)
            p.wait(s_, tv)
            i14 = s_.activation(sdo[:, r * 1024:(r + 1) * 1024], pp[:, :], AF.Exp)
            p.inc(i14, sc)
            sdo_sc.append(p.val(sc))

        p.wait(v, sc)
        d_sc = []
        mv_vv = []
        for r in range(8):
            pp = (ps1, ps2)[r % 2]
            par = r % 2
            p.wait(t, sc, sdo_sc[2 + r] if r < 2 else d_sc[r - 2])
            for i in range(8):
                h = r * 8 + i
                i15 = t.matmul(pp[:, i * 128:(i + 1) * 128],
                               stage1[0:4, h * 128:(h + 1) * 128],
                               stage2[0:4, h * 128:(h + 1) * 128], start=True, stop=True)
            p.inc(i15, tv)
            p.wait(s_, tv)
            if r >= 2:
                p.wait(s_, vv, mv_vv[r - 2])
            i16 = s_.activation(L_sb[:, par * 1024:(par + 1) * 1024], pp[:, :], AF.Exp)
            p.inc(i16, sc)
            d_sc.append(p.val(sc))
            p.wait(v, sc, d_sc[-1])
            i18 = v.scalar_tensor_tensor(
                M_sb[:, r * 1024:(r + 1) * 1024].rearrange("p (i l) -> p i l", i=8),
                L_sb[:, par * 1024:(par + 1) * 1024].rearrange("p (i l) -> p i l", i=8),
                1e30,
                G_m[:, :].unsqueeze(1).broadcast_to([128, 8, 128]),
                ALU.min, ALU.mult)
            p.inc(i18, vv)
            mv_vv.append(p.val(vv))

        p.wait(v, vv)
        v.drain()
        i19 = v.tensor_copy(run_bf[:, :], run[:, :])
        p.inc(i19, vv)
        p.wait(t, vv)
        p.wait(t, sc)
        yv = []
        for g4 in range(8):
            par = g4 % 2
            pa_y = (ps3, ps1)[par]
            pb_y = (ps4, ps2)[par]
            if g4 >= 2:
                p.wait(t, vv, yv[g4 - 2])
            for j in range(4):
                jt = g4 * 4 + j
                t.matmul(pa_y[:, j * 128:(j + 1) * 128], run_bf[:, jt * 128:(jt + 1) * 128],
                         xbc_sb[:, 33 * 128:34 * 128], start=True, stop=True)
            for j in range(4):
                jt = g4 * 4 + j
                t.matmul(pb_y[:, j * 128:(j + 1) * 128], dg[:, jt * 128:(jt + 1) * 128],
                         xbc_sb[:, jt * 128:(jt + 1) * 128], start=True, stop=False)
                h0 = jt * 2
                t.matmul(pb_y[0:64, j * 128:(j + 1) * 128], X_T[:, h0 * 64:(h0 + 1) * 64],
                         M_sb[:, h0 * 128:(h0 + 1) * 128], start=False, stop=False)
                i20 = t.matmul(pb_y[64:128, j * 128:(j + 1) * 128], X_T[:, (h0 + 1) * 64:(h0 + 2) * 64],
                               M_sb[:, (h0 + 1) * 128:(h0 + 2) * 128], start=False, stop=True)
            p.inc(i20, tv)
            p.wait(v, tv)
            v.tensor_tensor(tf[:, par * 512:(par + 1) * 512], pa_y[:, 0:512],
                            sdo[:, g4 * 512:(g4 + 1) * 512], ALU.mult)
            v.drain()
            v.tensor_add(tf[:, par * 512:(par + 1) * 512], tf[:, par * 512:(par + 1) * 512],
                         pb_y[:, 0:512])
            v.drain()
            i21 = v.tensor_mul(y3[:, g4 * 512:(g4 + 1) * 512],
                               tf[:, par * 512:(par + 1) * 512],
                               z_sb[:, g4 * 512:(g4 + 1) * 512])
            p.inc(i21, vv)
            yv.append(p.val(vv))

        for q in range(4):
            pp_s = (ps1, ps2)[q % 2]
            pp_g = (ps2, ps1)[q % 2]
            p.wait(t, vv)
            for hf in range(2):
                t.matmul(pp_s[:, hf * 512:(hf + 1) * 512], B_T[:, :],
                         Xw_T[:, q * 1024 + hf * 512: q * 1024 + (hf + 1) * 512],
                         start=True, stop=True)
                i22 = t.matmul(pp_g[:, hf * 512:(hf + 1) * 512], ones_bf[0:64, :],
                               gsel[0:64, q * 1024 + hf * 512: q * 1024 + (hf + 1) * 512],
                               start=True, stop=True)
            p.inc(i22, tv)
            p.wait(v, tv)
            v.drain()
            v.tensor_tensor(run[:, q * 1024:(q + 1) * 1024], run[:, q * 1024:(q + 1) * 1024],
                            pp_g[:, :], ALU.mult)
            v.drain()
            i23 = v.tensor_add(run[:, q * 1024:(q + 1) * 1024], run[:, q * 1024:(q + 1) * 1024],
                               pp_s[:, :])
            p.inc(i23, vv)

        v.drain()
        i24 = v.tensor_mul(sq[:, :], y3[:, :], y3[:, :])
        p.inc(i24, vv)
        p.wait(t, vv)
        for j in range(32):
            i25 = t.matmul(psR[0:1, :], ones_bf[:, 0:1], sq[:, j * 128:(j + 1) * 128],
                           start=(j == 0), stop=(j == 31))
        p.inc(i25, tv)
        p.wait(v, tv)
        v.tensor_scalar(rowbuf[0:1, 0:128], psR[0:1, :], 1.0 / 4096.0, 1e-5,
                        ALU.mult, ALU.add)
        v.drain()
        i26 = v.reciprocal(rowbuf[0:1, 128:256], rowbuf[0:1, 0:128])
        p.inc(i26, vv)
        p.wait(s_, vv)
        s_.activation(rowbuf[0:1, 256:384], rowbuf[0:1, 128:256], AF.Ln)
        s_.drain()
        i27 = s_.activation(rowbuf[0:1, 384:512], rowbuf[0:1, 256:384], AF.Exp, scale=0.5)
        p.inc(i27, sc)
        p.wait(t, sc)
        i28 = t.matmul(psR[:, :], ones_f32[0:1, :], rowbuf[0:1, 384:512], start=True, stop=True)
        p.inc(i28, tv)
        p.wait(v, tv)
        v.tensor_copy(inv_b[:, :], psR[:, :])
        v.drain()
        v.tensor_tensor(F_sb[:, :].rearrange("p (j l) -> p j l", j=32),
                        wn_sb[:, :].unsqueeze(2).broadcast_to([128, 32, 128]),
                        inv_b[:, :].unsqueeze(1).broadcast_to([128, 32, 128]),
                        ALU.mult)
        v.drain()
        if c >= 1:
            p.wait(v, st, p.val(st))
        i29 = v.tensor_mul(ynb[:, :], y3[:, :], F_sb[:, :])
        p.inc(i29, vv)
        p.wait(s_, vv)
        for j0 in (0, 16):
            i30 = s_.dma_start(out=yn_ap[j0 * 128:(j0 + 16) * 128, csl]
                               .rearrange("(j p) c -> p j c", p=128),
                               in_=ynb[:, j0 * 128:(j0 + 16) * 128]
                               .rearrange("p (j c) -> p j c", j=16))
            p.inc(i30, st, 16)
        p.inc(sy.nop(), cdone, 1) if False else p.inc(s_.nop(), cdone, 1)
    p.wait(sy, st)


def build_s6():
    p = P()
    io = {}
    xbc = p.dram("xBC_bf", [34 * 128, S], BF16, "ExternalInput")
    z = p.dram("z_silu", [4096, S], BF16, "ExternalInput")
    a = p.dram("A_arr", [64, S], F32, "ExternalInput")
    dp = p.dram("dp_hp", [128, 32], F32, "ExternalInput")
    wn = p.dram("wn_hp", [128, 32], F32, "ExternalInput")
    yn = p.dram("y_n", [4096, S], BF16, "ExternalOutput")
    load_consts(p, io)
    emit_ssd(p, io, "s6", xbc, z, a, dp, wn, yn)
    p.close()
    return p



def emit_l1_out(p, io, sfx, yn_ap, wout_ap, g2_ap, x1_ap, out_ap):
    nc = p.nc
    g_, v, t, s_, sy = nc.gpsimd, nc.vector, nc.tensor, nc.scalar, nc.sync

    yn_sb = p.sbuf(sfx + "_yn", [128, 32 * 2048], BF16)
    gx_g = p.sbuf(sfx + "_gxg", [128, 2 * 2048], BF16)
    gx_x = p.sbuf(sfx + "_gxx", [128, 2 * 2048], F32)
    t0f = p.sbuf(sfx + "_t0f", [128, 2 * 512], F32)
    ob = p.sbuf(sfx + "_ob", [128, 2 * 512], F32)

    pool = Pool(p, sfx, max_nk=32, cbw=512, dual=False)
    ld = p.sem(sfx + "_ld")
    xg = p.sem(sfx + "_xg")
    x1v = p.sem(sfx + "_x1v")
    st = [p.sem(sfx + "_st0"), p.sem(sfx + "_st1")]

    p.wait(g_, io["c_ld"])
    for j in range(32):
        p.inc(sy.dma_start(out=yn_sb[:, j * 2048:(j + 1) * 2048],
                           in_=yn_ap[j * 128:(j + 1) * 128, :]), ld, 16)
    ld_yn = p.val(ld)

    def on_m(mi):
        par = mi % 2
        if mi >= 2:
            p.wait(sy, x1v, p.val(x1v) - 4)
        p.inc(sy.dma_start(out=gx_g[:, par * 2048:(par + 1) * 2048],
                           in_=g2_ap[mi * 128:(mi + 1) * 128, :]), xg, 16)
        p.inc(sy.dma_start(out=gx_x[:, par * 2048:(par + 1) * 2048],
                           in_=x1_ap[mi * 128:(mi + 1) * 128, :]), xg, 16)

    def evac(mi, cb, pa, pb):
        n = pool.n_group - 1
        par2 = n % 2
        parm = mi % 2
        p.wait(v, pool.mm)
        p.wait(v, xg)
        if p.val(st[par2]) >= 16:
            p.wait(v, st[par2], p.val(st[par2]))
        i1 = v.tensor_mul(t0f[:, par2 * 512:(par2 + 1) * 512], pa,
                          gx_g[:, parm * 2048 + cb * 512: parm * 2048 + (cb + 1) * 512])
        p.inc(i1, pool.bk)
        v.drain()
        i2 = v.tensor_add(ob[:, par2 * 512:(par2 + 1) * 512],
                          t0f[:, par2 * 512:(par2 + 1) * 512],
                          gx_x[:, parm * 2048 + cb * 512: parm * 2048 + (cb + 1) * 512])
        p.inc(i2, x1v)
        p.wait(sy, x1v)
        p.inc(sy.dma_start(out=out_ap[mi * 128:(mi + 1) * 128, cb * 512:(cb + 1) * 512],
                           in_=ob[:, par2 * 512:(par2 + 1) * 512]), st[par2], 16)
        return 1

    mm_pass(p, pool, src_sb=yn_sb, src_cols=2048, n_k=32, n_m=16,
            Wimg=wout_ap, evac=evac, on_m=on_m, src_wait=(ld, ld_yn))
    p.wait(sy, st[0])
    p.wait(sy, st[1])


def build_s7():
    p = P()
    io = {}
    yn = p.dram("y_n", [4096, S], BF16, "ExternalInput")
    wout = p.dram("wout1_img", [2048, 4096], F32, "ExternalInput")
    g2 = p.dram("g2_bf", [2048, S], BF16, "ExternalInput")
    x1 = p.dram("x1", [2048, S], F32, "ExternalInput")
    out = p.dram("outT", [2048, S], F32, "ExternalOutput")
    load_consts(p, io)
    emit_l1_out(p, io, "s7", yn, wout, g2, x1, out)
    p.close()
    return p



_PROGS = {}


def _progs():
    if not _PROGS:
        _PROGS["s0"] = build_s0()
        _PROGS["s1"] = build_s1()
        _PROGS["s2"] = build_s2()
        _PROGS["s3"] = build_s3()
        _PROGS["s4"] = build_s4()
        _PROGS["s5"] = build_s5()
        _PROGS["s6"] = build_s6()
        _PROGS["s7"] = build_s7()
    return _PROGS


def _host_images(inp):
    im = {}
    im.update(consts_host())
    im["wr_img"] = host_wr_img(inp["L0_ffn_Wr"])
    im["win_img"] = _img_lhsT(inp["L0_attn_W_in"]).reshape(2048, 2048)
    im["wdec_img"] = _img_lhsT(inp["L0_attn_W_decay"]).reshape(2048, 2048)
    im["wgate_img"] = _img_lhsT(inp["L0_attn_W_gate"]).reshape(2048, 2048)
    im["wout_img"] = _img_lhsT(inp["L0_attn_W_out"]).reshape(2048, 2048)
    im["b_in_col"] = _bias_cols(inp["L0_attn_b_in"])
    im["b_dec_col"] = _bias_cols(inp["L0_attn_b_decay"])
    im["b_gate_col"] = _bias_cols(inp["L0_attn_b_gate"])
    im["b_out_col"] = _bias_cols(inp["L0_attn_b_out"])
    im["dec_bias_col"] = _bias_cols(inp["L0_attn_decay_bias"])
    im["wg_img"] = np.concatenate([_img_lhsT(inp["L0_ffn_Wg"][e]).reshape(2048, 2048)
                                   for e in range(8)], axis=0)
    im["wu_img"] = np.concatenate([_img_lhsT(inp["L0_ffn_Wu"][e]).reshape(2048, 2048)
                                   for e in range(8)], axis=0)
    im["wd_img"] = np.concatenate([_img_lhsT(inp["L0_ffn_Wd"][e]).reshape(2048, 2048)
                                   for e in range(8)], axis=0)
    im["w0_img"] = _img_lhsT(np.ascontiguousarray(inp["L1_ffn_W0"].T)).reshape(8192, 2048)
    im["w1_img"] = _img_lhsT(np.ascontiguousarray(inp["L1_ffn_W1"].T)).reshape(2048, 8192)
    im["b0_col"] = _bias_cols(inp["L1_ffn_bias0"])
    im["b1_col"] = _bias_cols(inp["L1_ffn_bias1"])
    WinT = np.ascontiguousarray(inp["L1_attn_W_in"].T)
    WinT = np.concatenate([WinT, np.zeros((2048, 64), np.float32)], axis=1)
    im["win1_img"] = _img_lhsT(WinT).reshape(67 * 128, 2048)
    wc = inp["L1_attn_W_conv"].reshape(4352, 4)
    im["taps"] = np.ascontiguousarray(wc.reshape(34, 128, 4).transpose(1, 0, 2).reshape(128, 136))
    im["bconv"] = _bias_cols(inp["L1_attn_b_conv"])
    im["dtb"] = np.ascontiguousarray(inp["L1_attn_dt_bias"][:, None])
    im["alog"] = np.ascontiguousarray(inp["L1_attn_A_log"][:, None])
    im["dp_hp"] = _bias_cols(np.repeat(inp["L1_attn_Dp"], 64))
    im["wn_hp"] = _bias_cols(inp["L1_attn_norm_w"])
    im["wout1_img"] = _img_lhsT(np.ascontiguousarray(inp["L1_attn_W_out"].T)).reshape(2048, 4096)
    return im


def _run(pg, maps):
    res = run_bass_kernel_spmd(pg.nc, maps, core_ids=list(range(NC)))
    return res.results


def kernel_multilaunch(**inputs):
    inp = {k: np.ascontiguousarray(np.asarray(v)) for k, v in inputs.items()}
    pr = _progs()
    im = _host_images(inp)
    x = inp["L0_attn_x"]
    xT = [np.ascontiguousarray(x[b].T) for b in range(NC)]
    C = {k: im["c_" + k] for k in
         ["ones_bf", "ones_f32", "ident_bf", "ident_f32", "triu01", "sel", "sel64"]}
    Cm = {"c_" + k: v for k, v in C.items()}

    r0 = _run(pr["s0"], [dict(Cm, xT=xT[b], wr_img=im["wr_img"]) for b in range(NC)])
    r1 = _run(pr["s1"], [dict(Cm, xn_bf=r0[b]["xn_bf"], w_fm=r0[b]["w_fm"],
                              wg_img=im["wg_img"], wu_img=im["wu_img"], wd_img=im["wd_img"])
                         for b in range(NC)])
    r2 = _run(pr["s2"], [dict(Cm, xn_bf=r0[b]["xn_bf"], xT=xT[b], g_bf=r1[b]["g_bf"],
                              win_img=im["win_img"], wdec_img=im["wdec_img"],
                              wgate_img=im["wgate_img"], wout_img=im["wout_img"],
                              b_in_col=im["b_in_col"], b_dec_col=im["b_dec_col"],
                              b_gate_col=im["b_gate_col"], b_out_col=im["b_out_col"],
                              dec_bias_col=im["dec_bias_col"]) for b in range(NC)])
    r3 = _run(pr["s3"], [dict(Cm, x1=r2[b]["x1"]) for b in range(NC)])
    r4 = _run(pr["s4"], [dict(Cm, xn2_bf=r3[b]["xn2_bf"], w0_img=im["w0_img"],
                              w1_img=im["w1_img"], b0_col=im["b0_col"], b1_col=im["b1_col"])
                         for b in range(NC)])
    r5 = _run(pr["s5"], [dict(Cm, xn2_bf=r3[b]["xn2_bf"], win_img=im["win1_img"],
                              taps=im["taps"], bconv=im["bconv"], dtb=im["dtb"],
                              alog=im["alog"]) for b in range(NC)])
    r6 = _run(pr["s6"], [dict(Cm, xBC_bf=r5[b]["xBC_bf"], z_silu=r5[b]["z_silu"],
                              A_arr=r5[b]["A_arr"], dp_hp=im["dp_hp"], wn_hp=im["wn_hp"])
                         for b in range(NC)])
    r7 = _run(pr["s7"], [dict(Cm, y_n=r6[b]["y_n"], wout1_img=im["wout1_img"],
                              g2_bf=r4[b]["g2_bf"], x1=r2[b]["x1"]) for b in range(NC)])
    out = np.stack([np.ascontiguousarray(r7[b]["outT"].T) for b in range(NC)], axis=0)
    return out



def build_fused():
    p = P()
    io = {}
    xT = p.dram("xT", [2048, S], F32, "ExternalInput")
    wr_img = p.dram("wr_img", [128, 128], F32, "ExternalInput")
    wg = p.dram("wg_img", [8 * 2048, 2048], F32, "ExternalInput")
    wu = p.dram("wu_img", [8 * 2048, 2048], F32, "ExternalInput")
    wd = p.dram("wd_img", [8 * 2048, 2048], F32, "ExternalInput")
    imgs0 = {k: p.dram(k + "_img", [2048, 2048], F32, "ExternalInput")
             for k in ["win", "wdec", "wgate", "wout"]}
    bias0 = {k: p.dram(k + "_col", [128, 16], F32, "ExternalInput")
             for k in ["b_in", "b_dec", "b_gate", "b_out", "dec_bias"]}
    w0 = p.dram("w0_img", [8192, 2048], F32, "ExternalInput")
    w1 = p.dram("w1_img", [2048, 8192], F32, "ExternalInput")
    b0 = p.dram("b0_col", [128, 64], F32, "ExternalInput")
    b1 = p.dram("b1_col", [128, 16], F32, "ExternalInput")
    win1 = p.dram("win1_img", [67 * 128, 2048], F32, "ExternalInput")
    taps = p.dram("taps", [128, 34 * 4], F32, "ExternalInput")
    bconv = p.dram("bconv", [128, 34], F32, "ExternalInput")
    dtb = p.dram("dtb", [64, 1], F32, "ExternalInput")
    alog = p.dram("alog", [64, 1], F32, "ExternalInput")
    dp = p.dram("dp_hp", [128, 32], F32, "ExternalInput")
    wn = p.dram("wn_hp", [128, 32], F32, "ExternalInput")
    wout1 = p.dram("wout1_img", [2048, 4096], F32, "ExternalInput")
    out = p.dram("outT", [2048, S], F32, "ExternalOutput")
    xn = p.dram("i_xn", [2048, S], BF16, "Internal")
    w_fm = p.dram("i_wfm", [8, S], F32, "Internal")
    g_bf = p.dram("i_g", [2048, S], BF16, "Internal")
    x1 = p.dram("i_x1", [2048, S], F32, "Internal")
    xn2 = p.dram("i_xn2", [2048, S], BF16, "Internal")
    g2 = p.dram("i_g2", [2048, S], BF16, "Internal")
    z = p.dram("i_z", [4096, S], BF16, "Internal")
    xbc = p.dram("i_xbc", [34 * 128, S], BF16, "Internal")
    a_arr = p.dram("i_a", [64, S], F32, "Internal")
    yn = p.dram("i_yn", [4096, S], BF16, "Internal")

    load_consts(p, io)
    bar = p.sem("bar")
    with p.scope():
        emit_ln_router(p, io, "s0", xT, xn, router=True, wr_img_ap=wr_img, w_ap=w_fm)
    p.barrier(bar)
    with p.scope():
        emit_moe(p, io, "s1", xn, w_fm, wg, wu, wd, g_bf)
    p.barrier(bar)
    with p.scope():
        emit_l0_attn(p, io, "s2", xn, xT, g_bf, x1, imgs0, bias0)
    p.barrier(bar)
    with p.scope():
        emit_ln_router(p, io, "s3", x1, xn2, router=False)
    p.barrier(bar)
    with p.scope():
        emit_ffn(p, io, "s4", xn2, w0, w1, b0, b1, g2)
    p.barrier(bar)
    with p.scope():
        emit_zxbcdt(p, io, "s5", xn2, win1, taps, bconv, dtb, alog, z, xbc, a_arr)
    p.barrier(bar)
    with p.scope():
        emit_ssd(p, io, "s6", xbc, z, a_arr, dp, wn, yn)
    p.barrier(bar)
    with p.scope():
        emit_l1_out(p, io, "s7", yn, wout1, g2, x1, out)
    p.close()
    return p


_FUSED = []


def kernel_fused(**inputs):
    inp = {k: np.ascontiguousarray(np.asarray(v)) for k, v in inputs.items()}
    if not _FUSED:
        _FUSED.append(build_fused())
    pg = _FUSED[0]
    im = _host_images(inp)
    x = inp["L0_attn_x"]
    maps = []
    for b in range(NC):
        m = {k: v for k, v in im.items()}
        m["xT"] = np.ascontiguousarray(x[b].T)
        maps.append(m)
    res = run_bass_kernel_spmd(pg.nc, maps, core_ids=list(range(NC)))
    return np.stack([np.ascontiguousarray(res.results[b]["outT"].T) for b in range(NC)], axis=0)


kernel = kernel_fused


# version 08 (final)
# speedup vs baseline: 1.2451x; 1.2451x over previous
import numpy as np
import ml_dtypes
import concourse.bass as bass
import concourse.mybir as mybir
from concourse.bass_utils import run_bass_kernel_spmd

F32 = mybir.dt.float32
BF16 = mybir.dt.bfloat16
AF = mybir.ActivationFunctionType
ALU = mybir.AluOpType

S = 2048
NC = 8
BF = ml_dtypes.bfloat16


def _img_lhsT(W):
    K, M = W.shape
    assert K % 128 == 0 and M % 128 == 0, (K, M)
    nk, nm = K // 128, M // 128
    W4 = W.reshape(nk, 128, nm, 128)
    return np.ascontiguousarray(W4.transpose(2, 1, 0, 3).reshape(nm, 128, nk * 128))


def _bias_cols(v):
    n = v.shape[0] // 128
    return np.ascontiguousarray(v.reshape(n, 128).T)



class P:

    def __init__(self):
        self.nc = bass.Bass("TRN2", target_bir_lowering=False)
        self.ctx = []
        self.sem_ctx = []
        self.sems = {}

    def dram(self, name, shape, dtype, kind):
        return self.nc.dram_tensor(name, list(shape), dtype, kind=kind).ap()

    def sbuf(self, name, shape, dtype):
        cm = self.nc.sbuf_tensor(name, list(shape), dtype)
        t = cm.__enter__()
        self.ctx.append(cm)
        return t

    def psum(self, shape, dtype=F32):
        cm = self.nc.psum_tensor(list(shape), dtype)
        t = cm.__enter__()
        self.ctx.append(cm)
        return t

    def sem(self, name):
        cm = self.nc.semaphore(name)
        h = cm.__enter__()
        self.sem_ctx.append(cm)
        self.sems[name] = [h, 0]
        return name

    def scope(self):
        p = self

        class _Scope:
            def __enter__(self):
                self.n0 = len(p.ctx)
                return self

            def __exit__(self, *a):
                for cm in reversed(p.ctx[self.n0:]):
                    cm.__exit__(None, None, None)
                del p.ctx[self.n0:]
                return False
        return _Scope()

    def barrier(self, bar):
        nc = self.nc
        engs = [nc.gpsimd, nc.vector, nc.tensor, nc.scalar, nc.sync]
        for e in engs:
            e.drain()
            self.inc(e.nop(), bar, 1)
        tgt = self.val(bar)
        for e in engs:
            e.wait_ge(self.sems[bar][0], tgt)

    def inc(self, inst, name, amt=1):
        s = self.sems[name]
        inst.then_inc(s[0], amt)
        s[1] += amt
        return s[1]

    def val(self, name):
        return self.sems[name][1]

    def wait(self, eng, name, val=None):
        s = self.sems[name]
        eng.wait_ge(s[0], s[1] if val is None else val)

    def close(self):
        for cm in reversed(self.ctx):
            cm.__exit__(None, None, None)
        for cm in reversed(self.sem_ctx):
            cm.__exit__(None, None, None)
        self.ctx = []
        self.sem_ctx = []

    @property
    def e(self):
        nc = self.nc
        return dict(gpsimd=nc.gpsimd, vector=nc.vector, tensor=nc.tensor,
                    scalar=nc.scalar, sync=nc.sync)


def load_consts(p, io):
    nc = p.nc
    g = nc.gpsimd
    ld = p.sem("c_ld")
    specs = [
        ("ones_bf", [128, 128], BF16), ("ones_f32", [128, 128], F32),
        ("ident_bf", [128, 128], BF16), ("ident_f32", [128, 128], F32),
        ("triu01", [128, 128], BF16),
    ]
    for name, shape, dt in specs:
        ap = p.dram("c_" + name, shape, dt, "ExternalInput")
        sb = p.sbuf("c_" + name + "_sb", shape, dt)
        p.inc(g.dma_start(out=sb[:, :], in_=ap), ld, 16)
        io[name + "_sb"] = sb
    io["sel_ap"] = p.dram("c_sel", [64, 32 * 128], BF16, "ExternalInput")
    io["sel64_ap"] = p.dram("c_sel64", [64, 4096], BF16, "ExternalInput")
    for name, shape in [("iota8", [128, 8]), ("iota128", [128, 1])]:
        ap = p.dram("c_" + name, shape, F32, "ExternalInput")
        sb = p.sbuf("c_" + name + "_sb", shape, F32)
        p.inc(g.dma_start(out=sb[:, :], in_=ap), ld, 16)
        io[name + "_sb"] = sb
    io["c_ld"] = ld
    return ld


def consts_host():
    ones_bf = np.ones((128, 128), dtype=BF)
    ones_f32 = np.ones((128, 128), dtype=np.float32)
    ident_bf = np.eye(128, dtype=BF)
    ident_f32 = np.eye(128, dtype=np.float32)
    sl = np.arange(128)
    triu01 = (sl[:, None] <= sl[None, :]).astype(BF)
    sel = np.zeros((32, 64, 128), dtype=BF)
    for j in range(32):
        for m in range(128):
            sel[j, (128 * j + m) // 64, m] = 1
    sel = np.ascontiguousarray(sel.transpose(1, 0, 2).reshape(64, 32 * 128))
    sel64 = np.zeros((64, 4096), dtype=BF)
    for h in range(64):
        sel64[h, h * 64:(h + 1) * 64] = 1
    iota8 = np.tile(np.arange(8, dtype=np.float32), (128, 1))
    iota128 = np.arange(128, dtype=np.float32).reshape(128, 1)
    return {"c_ones_bf": ones_bf, "c_ones_f32": ones_f32, "c_ident_bf": ident_bf,
            "c_ident_f32": ident_f32, "c_triu01": triu01, "c_sel": sel,
            "c_sel64": sel64, "c_iota8": iota8, "c_iota128": iota128}



def emit_ln_router(p, io, sfx, src_ap, xn_ap, router=False, wr_img_ap=None, w_ap=None):
    nc = p.nc
    g, v, t, s_, sy = nc.gpsimd, nc.vector, nc.tensor, nc.scalar, nc.sync
    DT = 16
    W = 2048

    x_sb = p.sbuf(sfx + "_x", [128, 2 * W], F32)
    xb_sb = p.sbuf(sfx + "_xb", [128, DT * W], BF16)
    sq_t = p.sbuf(sfx + "_sq", [128, 2 * W], BF16)
    rb_t = p.sbuf(sfx + "_rb", [128, 2 * W], BF16)
    rows = p.sbuf(sfx + "_rows", [1, 4 * W], F32)
    invb = p.sbuf(sfx + "_invb", [128, W], BF16)
    minvb = p.sbuf(sfx + "_minvb", [128, W], BF16)
    xn_buf = p.sbuf(sfx + "_xnb", [128, 2 * W], BF16)
    tmp = p.sbuf(sfx + "_tmp", [128, 2 * W], BF16)
    ps_all = p.psum([128, W])
    ps_b = p.psum([128, 1024])
    if router:
        wr_sb = p.sbuf(sfx + "_wr", [128, 128], F32)
        csum8 = p.sbuf(sfx + "_csum8", [128, 8], F32)
        rstage = p.sbuf(sfx + "_rstage", [128, W], F32)
        tok = p.sbuf(sfx + "_tok", [128, 128], F32)
        t8 = p.sbuf(sfx + "_t8", [128, 64], F32)
        c1 = p.sbuf(sfx + "_c1", [128, 8], F32)
        wtr = p.sbuf(sfx + "_wtr", [128, 128], F32)
        w_sb = p.sbuf(sfx + "_wsb", [8, S], F32)
        csr = p.sbuf(sfx + "_csr", [1, 8], F32)
        m1_sb = p.sbuf(sfx + "_m1", [8, S], F32)
        m2_sb = p.sbuf(sfx + "_m2", [8, S], F32)
        rstage2 = p.sbuf(sfx + "_rstage2", [128, W], F32)
        tok2 = p.sbuf(sfx + "_tok2", [128, 128], F32)
        w12 = p.sbuf(sfx + "_w12", [128, 32], F32)
        offs_i = p.sbuf(sfx + "_offs", [128, 32], mybir.dt.int32)
        zeros8 = p.sbuf(sfx + "_z8", [8, S], F32)
        ps_t = p.psum([128, 128])
        ps_c = p.psum([1, 128])

    ld = [p.sem(sfx + "_ld0"), p.sem(sfx + "_ld1")]
    wld = p.sem(sfx + "_wld")
    mm = p.sem(sfx + "_mm")
    ev = p.sem(sfx + "_ev")
    sc = p.sem(sfx + "_sc")
    st = [p.sem(sfx + "_st0"), p.sem(sfx + "_st1")]

    ones_bf = io["ones_bf_sb"]
    ones_f32 = io["ones_f32_sb"]
    ident_f32 = io["ident_f32_sb"]
    p.wait(g, io["c_ld"])
    p.wait(t, io["c_ld"])
    p.wait(v, io["c_ld"])
    p.wait(s_, io["c_ld"])

    if router:
        p.inc(g.dma_start(out=wr_sb[:, :], in_=wr_img_ap), wld, 16)
        p.wait(t, wld)
        i1 = t.matmul(ps_c[0:1, :], ones_f32[:, 0:1], wr_sb[:, :], start=True, stop=True)
        p.inc(i1, mm)
        p.wait(v, mm)
        i2 = v.tensor_reduce(csr[0:1, 0:8],
                             ps_c[0:1, :].rearrange("p (j e) -> p e j", e=8),
                             mybir.AxisListType.X, ALU.add)
        p.inc(i2, ev)
        p.wait(t, ev)
        i3 = t.matmul(ps_t[:, 0:8], ones_f32[0:1, :], csr[0:1, 0:8], start=True, stop=True)
        p.inc(i3, mm)
        p.wait(v, mm)
        p.inc(v.tensor_copy(csum8[:, :], ps_t[:, 0:8]), ev)

    mm_t = []
    for j in range(DT):
        par = j % 2
        if j >= 2:
            p.wait(sy, mm, mm_t[j - 2])
        p.inc(sy.dma_start(out=x_sb[:, par * W:(par + 1) * W],
                           in_=src_ap[j * 128:(j + 1) * 128, :]), ld[par], 16)
        p.wait(s_, ld[par], p.val(ld[par]))
        i1 = s_.copy(xb_sb[:, j * W:(j + 1) * W], x_sb[:, par * W:(par + 1) * W])
        p.inc(i1, sc)
        p.wait(v, sc)
        v.tensor_mul(sq_t[:, par * W:(par + 1) * W], xb_sb[:, j * W:(j + 1) * W],
                     xb_sb[:, j * W:(j + 1) * W])
        i2 = v.tensor_sub(rb_t[:, par * W:(par + 1) * W], x_sb[:, par * W:(par + 1) * W],
                          xb_sb[:, j * W:(j + 1) * W])
        p.inc(i2, ev)
        p.wait(t, ev)
        for nb in range(4):
            nsl = slice(nb * 512, (nb + 1) * 512)
            psl = slice(par * W + nb * 512, par * W + (nb + 1) * 512)
            xsl = slice(j * W + nb * 512, j * W + (nb + 1) * 512)
            t.matmul(ps_all[0:1, nsl], ones_bf[:, 0:1], xb_sb[:, xsl],
                     start=(j == 0), stop=False)
            t.matmul(ps_all[0:1, nsl], ones_bf[:, 0:1], rb_t[:, psl],
                     start=False, stop=(j == DT - 1))
            i3 = t.matmul(ps_all[32:33, nsl], ones_bf[:, 0:1], sq_t[:, psl],
                          start=(j == 0), stop=(j == DT - 1))
            if router:
                i3 = t.matmul(ps_all[64:72, nsl], wr_sb[:, j * 8:(j + 1) * 8],
                              x_sb[:, psl], start=(j == 0), stop=(j == DT - 1))
        p.inc(i3, mm)
        mm_t.append(p.val(mm))

    def R(k):
        return rows[0:1, k * W:(k + 1) * W]
    p.wait(v, mm)
    v.tensor_scalar_mul(R(0), ps_all[0:1, :], 1.0 / 2048.0)
    v.tensor_scalar_mul(R(1), ps_all[32:33, :], 1.0 / 2048.0)
    v.drain()
    v.tensor_mul(R(2), R(0), R(0))
    v.drain()
    v.tensor_sub(R(1), R(1), R(2))
    v.drain()
    i3 = v.tensor_scalar_add(R(1), R(1), 1e-5)
    p.inc(i3, ev)
    p.wait(s_, ev)
    s_.activation(R(2), R(1), AF.Ln)
    s_.drain()
    i4 = s_.activation(R(2), R(2), AF.Exp, scale=-0.5)
    p.inc(i4, sc)
    p.wait(v, sc)
    i5 = v.tensor_mul(R(3), R(0), R(2))
    p.inc(i5, ev)
    p.wait(t, ev)
    ev_b = []
    for nb in range(4):
        nsl = slice(nb * 512, (nb + 1) * 512)
        if nb >= 1:
            p.wait(t, ev, ev_b[-1])
        t.matmul(ps_b[:, 0:512], ones_f32[0:1, :], R(2)[0:1, nsl], start=True, stop=True)
        i6 = t.matmul(ps_b[:, 512:1024], ones_f32[0:1, :], R(3)[0:1, nsl], start=True, stop=True)
        p.inc(i6, mm)
        p.wait(v, mm)
        v.tensor_copy(invb[:, nsl], ps_b[:, 0:512])
        i7 = v.tensor_copy(minvb[:, nsl], ps_b[:, 512:1024])
        p.inc(i7, ev)
        ev_b.append(p.val(ev))

    for j in range(DT):
        par = j % 2
        if j >= 2:
            p.wait(v, st[par], p.val(st[par]))
        v.drain()
        v.tensor_mul(tmp[:, par * W:(par + 1) * W], xb_sb[:, j * W:(j + 1) * W], invb[:, :])
        v.drain()
        i8 = v.tensor_sub(xn_buf[:, par * W:(par + 1) * W], tmp[:, par * W:(par + 1) * W],
                          minvb[:, :])
        p.inc(i8, ev)
        p.wait(sy, ev)
        p.inc(sy.dma_start(out=xn_ap[j * 128:(j + 1) * 128, :],
                           in_=xn_buf[:, par * W:(par + 1) * W]), st[par], 16)

    if router:
        p.wait(v, mm, mm_t[-1])
        v.tensor_copy(rstage[0:8, :], ps_all[64:72, :])
        v.tensor_copy(rstage[32:33, :], R(0))
        i9 = v.tensor_copy(rstage[64:65, :], R(2))
        p.inc(i9, ev)
        for gi in range(W // 128):
            p.wait(t, ev)
            i10 = t.transpose(ps_t[:, :], rstage[:, gi * 128:(gi + 1) * 128], ident_f32[:, :])
            p.inc(i10, mm)
            p.wait(v, mm)
            p.inc(v.tensor_copy(tok[:, :], ps_t[:, :]), ev)
            lg = tok[:, 0:8]
            m_c = tok[:, 32:33]
            i_c = tok[:, 64:65]
            v.drain()
            v.scalar_tensor_tensor(t8[:, 0:8], csum8[:, :], m_c, lg, ALU.mult, ALU.subtract)
            v.drain()
            v.tensor_scalar(t8[:, 8:16], t8[:, 0:8], i_c, -1.0, ALU.mult, ALU.mult)
            v.drain()
            v.tensor_reduce(c1[:, 0:1], t8[:, 8:16], mybir.AxisListType.X, ALU.max)
            v.drain()
            i11 = v.tensor_scalar_mul(c1[:, 1:2], c1[:, 0:1], -1.0)
            p.inc(i11, ev)
            p.wait(s_, ev)
            i12 = s_.activation(t8[:, 16:24], t8[:, 8:16], AF.Exp, bias=c1[:, 1:2])
            p.inc(i12, sc)
            p.wait(v, sc)
            e8 = t8[:, 16:24]
            v.tensor_reduce(c1[:, 2:3], e8, mybir.AxisListType.X, ALU.max)
            v.drain()
            v.tensor_single_scalar(wtr[:, 32:40], e8, c1[:, 2:3], ALU.is_ge)
            v.drain()
            v.tensor_mul(t8[:, 32:40], e8, wtr[:, 32:40])
            v.drain()
            v.tensor_sub(t8[:, 40:48], e8, t8[:, 32:40])
            v.drain()
            v.tensor_reduce(c1[:, 3:4], t8[:, 40:48], mybir.AxisListType.X, ALU.max)
            v.drain()
            v.tensor_single_scalar(t8[:, 24:32], e8, c1[:, 3:4], ALU.is_ge)
            v.drain()
            v.tensor_sub(wtr[:, 64:72], t8[:, 24:32], wtr[:, 32:40])
            v.tensor_mul(t8[:, 32:40], e8, t8[:, 24:32])
            v.drain()
            v.tensor_reduce(c1[:, 4:5], t8[:, 32:40], mybir.AxisListType.X, ALU.add)
            v.drain()
            v.reciprocal(c1[:, 5:6], c1[:, 4:5])
            v.drain()
            i13 = v.tensor_scalar_mul(wtr[:, 0:8], t8[:, 32:40], c1[:, 5:6])
            v.drain()
            v.tensor_reduce(w12[:, gi:gi + 1], wtr[:, 0:8], mybir.AxisListType.X, ALU.max)
            v.tensor_reduce(c1[:, 6:7], wtr[:, 0:8], mybir.AxisListType.X, ALU.add)
            v.drain()
            i13 = v.tensor_sub(w12[:, 16 + gi:17 + gi], c1[:, 6:7], w12[:, gi:gi + 1])
            p.inc(i13, ev)
            p.wait(t, ev)
            i14 = t.transpose(ps_t[:, :], wtr[:, :], ident_f32[:, :])
            p.inc(i14, mm)
            p.wait(v, mm)
            v.tensor_copy(w_sb[0:8, gi * 128:(gi + 1) * 128], ps_t[0:8, :])
            v.tensor_copy(m1_sb[0:8, gi * 128:(gi + 1) * 128], ps_t[32:40, :])
            p.inc(v.tensor_copy(m2_sb[0:8, gi * 128:(gi + 1) * 128], ps_t[64:72, :]), ev)
        p.wait(sy, ev)
        p.inc(sy.dma_start(out=w_ap, in_=w_sb[:, :]), st[0], 16)

        iota8 = io["iota8_sb"]
        v.memset(zeros8[0:8, :], 0.0)
        v.drain()
        v.tensor_add(csum8[0:8, 0:8], csum8[0:8, 0:8], zeros8[0:8, 0:8]) if False else None
        v.tensor_add(m2_sb[0:8, :], m2_sb[0:8, :], m1_sb[0:8, :])
        v.drain()
        v.tensor_tensor_scan(rstage[0:8, :], m2_sb[0:8, :], zeros8[0:8, :], 0.0,
                             ALU.add, ALU.add)
        v.drain()
        v.tensor_sub(m2_sb[0:8, :], m2_sb[0:8, :], m1_sb[0:8, :])
        v.drain()
        v.tensor_copy(rstage[64:72, :], m1_sb[0:8, :])
        i20 = v.tensor_copy(rstage2[0:8, :], m2_sb[0:8, :])
        p.inc(i20, ev)
        for gi in range(W // 128):
            p.wait(t, ev)
            i21 = t.transpose(ps_t[:, :], rstage[:, gi * 128:(gi + 1) * 128], ident_f32[:, :])
            p.inc(i21, mm)
            p.wait(v, mm)
            p.inc(v.tensor_copy(tok[:, :], ps_t[:, :]), ev)
            p.wait(t, ev)
            i22 = t.transpose(ps_t[:, :], rstage2[:, gi * 128:(gi + 1) * 128], ident_f32[:, :])
            p.inc(i22, mm)
            p.wait(v, mm)
            p.inc(v.tensor_copy(tok2[:, :], ps_t[:, :]), ev)
            v.drain()
            for rank in range(2):
                mT = tok[:, 64:72] if rank == 0 else tok2[:, 0:8]
                preT = tok[:, 0:8]
                v.tensor_mul(t8[:, 0:8], mT, preT)
                v.tensor_mul(t8[:, 8:16], mT, iota8[:, :])
                v.drain()
                v.tensor_reduce(c1[:, 0:1], t8[:, 0:8], mybir.AxisListType.X, ALU.add)
                v.tensor_reduce(c1[:, 1:2], t8[:, 8:16], mybir.AxisListType.X, ALU.add)
                v.drain()
                v.scalar_tensor_tensor(c1[:, 2:3], c1[:, 1:2], 768.0, c1[:, 0:1],
                                       ALU.mult, ALU.add)
                v.tensor_single_scalar(c1[:, 3:4], c1[:, 0:1], 768.0, ALU.is_gt)
                v.drain()
                v.scalar_tensor_tensor(c1[:, 4:5], c1[:, 3:4], 65536.0, c1[:, 2:3],
                                       ALU.mult, ALU.add)
                v.drain()
                v.tensor_scalar_add(c1[:, 5:6], c1[:, 4:5], -1.0)
                v.drain()
                i23 = v.tensor_copy(offs_i[:, rank * 16 + gi:rank * 16 + gi + 1], c1[:, 5:6])
                p.inc(i23, ev)
        p.wait(sy, ev)
        p.inc(sy.dma_start(out=io["r_off_ap"], in_=offs_i[:, :]), st[0], 16)
        p.inc(sy.dma_start(out=io["r_w_ap"], in_=w12[:, :]), st[0], 16)
    p.wait(sy, st[0])
    p.wait(sy, st[1])



def build_s0(router=True, sfx="s0"):
    p = P()
    io = {}
    xT = p.dram("xT", [2048, S], F32, "ExternalInput")
    xn = p.dram("xn_bf", [2048, S], BF16, "ExternalOutput")
    wr_img = p.dram("wr_img", [128, 128], F32, "ExternalInput") if router else None
    w_fm = p.dram("w_fm", [8, S], F32, "ExternalOutput") if router else None
    if router:
        io["r_off_ap"] = p.dram("r_off", [128, 32], mybir.dt.int32, "ExternalOutput")
        io["r_w_ap"] = p.dram("r_w", [128, 32], F32, "ExternalOutput")
    load_consts(p, io)
    emit_ln_router(p, io, sfx, xT, xn, router=router, wr_img_ap=wr_img, w_ap=w_fm)
    p.close()
    return p


def host_wr_img(Wr):
    return np.ascontiguousarray(Wr.reshape(16, 128, 8).transpose(1, 0, 2).reshape(128, 128))






class Pool:

    def __init__(self, p, sfx, max_nk, cbw=512, dual=True, depth=3):
        self.p = p
        self.cbw = cbw
        self.max_nk = max_nk
        self.depth = depth
        self.nstreams = 2 if dual else 1
        self.wbuf = p.sbuf(sfx + "_wb", [128, depth * self.nstreams * max_nk * 128], BF16)
        self.ps_a = p.psum([128, 2 * 512])
        self.ps_b = p.psum([128, 2 * 512]) if dual else None
        self.ldw = [p.sem(sfx + "_ldw%d" % i) for i in range(depth)]
        self.mm = p.sem(sfx + "_mm")
        self.bk = p.sem(sfx + "_bk")
        self.n_group = 0
        self.bk_after = []
        self.wpar_last_group = [None] * depth
        self.wpar = 0

    def wv(self, par, wi):
        off = (par * self.nstreams + wi) * self.max_nk * 128
        return self.wbuf[:, off:off + self.max_nk * 128]


def mm_pass(p, pool, *, src_sb, src_cols, n_k, n_m, Wimg, evac, W2img=None,
            on_m=None, src2_sb=None, cbw=None, src_wait=None):
    nc = p.nc
    g, t = nc.gpsimd, nc.tensor
    cbw = cbw or pool.cbw
    ncb = src_cols // cbw
    nw = 2 if W2img is not None else 1
    assert n_k <= pool.max_nk and cbw <= pool.cbw
    if src_wait is not None:
        p.wait(nc.tensor, src_wait[0], src_wait[1])

    D = pool.depth
    for m in range(n_m + D - 1):
        if m < n_m:
            par = pool.wpar
            pool.wpar = (pool.wpar + 1) % D
            prev = pool.wpar_last_group[par]
            if prev is not None:
                p.wait(g, pool.mm, prev + 1)
            p.inc(g.dma_start(out=pool.wv(par, 0)[:, 0:n_k * 128],
                              in_=Wimg[m * 128:(m + 1) * 128, :]), pool.ldw[par], 16)
            if nw == 2:
                p.inc(g.dma_start(out=pool.wv(par, 1)[:, 0:n_k * 128],
                                  in_=W2img[m * 128:(m + 1) * 128, :]), pool.ldw[par], 16)
            pool._pending = getattr(pool, "_pending", {})
            pool._pending[m] = (par, p.val(pool.ldw[par]))
        if m < D - 1:
            continue
        mi = m - (D - 1)
        par, ldw_tgt = pool._pending.pop(mi)
        if on_m is not None:
            on_m(mi)
        p.wait(t, pool.ldw[par], ldw_tgt)
        for cb in range(ncb):
            n = pool.n_group
            b = n % 2
            if n >= 2:
                p.wait(t, pool.bk, pool.bk_after[n - 2])
            pa = pool.ps_a[:, b * 512: b * 512 + cbw]
            for k in range(n_k):
                i1 = t.matmul(pa, pool.wv(par, 0)[:, k * 128:(k + 1) * 128],
                              src_sb[:, k * src_cols + cb * cbw:k * src_cols + (cb + 1) * cbw],
                              start=(k == 0), stop=(k == n_k - 1))
            pb = None
            if nw == 2:
                pb = pool.ps_b[:, b * 512: b * 512 + cbw]
                s2 = src2_sb if src2_sb is not None else src_sb
                for k in range(n_k):
                    i1 = t.matmul(pb, pool.wv(par, 1)[:, k * 128:(k + 1) * 128],
                                  s2[:, k * src_cols + cb * cbw:k * src_cols + (cb + 1) * cbw],
                                  start=(k == 0), stop=(k == n_k - 1))
            p.inc(i1, pool.mm)
            pool.n_group += 1
            pool.wpar_last_group[par] = pool.n_group - 1
            ninc = evac(mi, cb, pa, pb)
            pool.bk_after.append(p.val(pool.bk))
            assert ninc >= 1



def emit_l0_attn(p, io, sfx, xn_ap, xT_ap, g_ap, x1_ap, imgs, bias_aps):
    nc = p.nc
    g_, v, t, s_, sy = nc.gpsimd, nc.vector, nc.tensor, nc.scalar, nc.sync
    H = 1024
    DT = 16
    CB = 256

    xn_sb = p.sbuf(sfx + "_xn", [128, DT * H], BF16)
    d_sb = p.sbuf(sfx + "_d", [128, DT * H], BF16)
    u_sb = p.sbuf(sfx + "_u", [128, DT * H], BF16)
    gt_sb = p.sbuf(sfx + "_gt", [128, DT * H], BF16)
    y_sb = xn_sb
    stA = p.sbuf(sfx + "_stA", [128, DT * CB], BF16)
    stB = p.sbuf(sfx + "_stB", [128, DT * CB], BF16)
    tt0 = p.sbuf(sfx + "_tt0", [128, 2 * 512], BF16)
    tu0 = p.sbuf(sfx + "_tu0", [128, 2 * 512], F32)
    t0f = p.sbuf(sfx + "_t0f", [128, 2 * 512], F32)
    ob = p.sbuf(sfx + "_ob", [128, 2 * 512], F32)
    gx_g = p.sbuf(sfx + "_gxg", [128, 2 * H], BF16)
    gx_x = p.sbuf(sfx + "_gxx", [128, 2 * H], F32)
    bcols = p.sbuf(sfx + "_bc", [128, 5 * 16], F32)

    pool = Pool(p, sfx, max_nk=16, cbw=512, dual=True)
    ld = p.sem(sfx + "_ld")
    bld = p.sem(sfx + "_bld")
    vv = p.sem(sfx + "_vv")
    xg = p.sem(sfx + "_xg")
    x1v = p.sem(sfx + "_x1v")
    st = [p.sem(sfx + "_st0"), p.sem(sfx + "_st1")]

    p.wait(t, io["c_ld"])
    for i, apn in enumerate(["b_in", "b_dec", "b_gate", "b_out", "dec_bias"]):
        p.inc(g_.dma_start(out=bcols[:, i * 16:(i + 1) * 16], in_=bias_aps[apn]), bld, 16)
    p.wait(v, bld)
    i0 = v.tensor_add(bcols[:, 16:32], bcols[:, 16:32], bcols[:, 64:80])
    p.inc(i0, vv)
    vv_bias = p.val(vv)
    p.wait(s_, vv, vv_bias)

    def bc(i, m):
        return bcols[:, i * 16 + m:i * 16 + m + 1]

    for hb in range(2):
        hs = slice(hb * H, (hb + 1) * H)
        if hb > 0:
            p.wait(sy, st[0])
            p.wait(sy, st[1])
        for j in range(DT):
            p.inc(sy.dma_start(out=xn_sb[:, j * H:(j + 1) * H],
                               in_=xn_ap[j * 128:(j + 1) * 128, hs]), ld, 16)
        ld_xn = p.val(ld)
        if hb > 0:
            p.wait(s_, vv)

        def evac1(mi, cb, pa, pb):
            n = pool.n_group - 1
            par = n % 2
            b0 = p.val(pool.bk)
            p.wait(s_, pool.mm)
            if n >= 2:
                p.wait(s_, vv, p.val(vv) - 1)
            i1 = s_.activation(tt0[:, par * 512:(par + 1) * 512], pa, AF.Silu, bias=bc(0, mi))
            p.inc(i1, pool.bk)
            i2 = s_.activation(d_sb[:, mi * H + cb * 512: mi * H + (cb + 1) * 512], pb,
                               AF.Sigmoid, bias=bc(1, mi))
            p.inc(i2, pool.bk)
            p.wait(v, pool.bk)
            v.tensor_scalar(tu0[:, par * 512:(par + 1) * 512],
                            d_sb[:, mi * H + cb * 512: mi * H + (cb + 1) * 512],
                            -1.0, 1.0, ALU.mult, ALU.add)
            v.drain()
            i3 = v.tensor_mul(u_sb[:, mi * H + cb * 512: mi * H + (cb + 1) * 512],
                              tu0[:, par * 512:(par + 1) * 512],
                              tt0[:, par * 512:(par + 1) * 512])
            p.inc(i3, vv)
            return p.val(pool.bk) - b0

        mm_pass(p, pool, src_sb=xn_sb, src_cols=H, n_k=16, n_m=16,
                Wimg=imgs["win"], W2img=imgs["wdec"], evac=evac1, src_wait=(ld, ld_xn))

        def evac2(mi, cb, pa, pb):
            b0 = p.val(pool.bk)
            p.wait(s_, pool.mm)
            i1 = s_.activation(gt_sb[:, mi * H + cb * 512: mi * H + (cb + 1) * 512], pa,
                               AF.Sigmoid, bias=bc(2, mi))
            p.inc(i1, pool.bk)
            return p.val(pool.bk) - b0

        mm_pass(p, pool, src_sb=xn_sb, src_cols=H, n_k=16, n_m=16,
                Wimg=imgs["wgate"], evac=evac2, src_wait=(ld, ld_xn))

        p.wait(v, pool.bk)
        p.wait(v, vv)
        for cb in range(H // CB):
            gcb = hb * (H // CB) + cb
            cA, cB = (stA, stB) if gcb % 2 == 0 else (stB, stA)
            v.drain()
            for j in range(DT):
                dsl = d_sb[:, j * H + cb * CB: j * H + (cb + 1) * CB]
                usl = u_sb[:, j * H + cb * CB: j * H + (cb + 1) * CB]
                init = 0.0 if gcb == 0 else cB[:, j * CB + CB - 1: j * CB + CB]
                v.tensor_tensor_scan(cA[:, j * CB:(j + 1) * CB], dsl, usl, init,
                                     ALU.mult, ALU.add)
            v.drain()
            for j in range(DT):
                i1 = v.tensor_mul(y_sb[:, j * H + cb * CB: j * H + (cb + 1) * CB],
                                  gt_sb[:, j * H + cb * CB: j * H + (cb + 1) * CB],
                                  cA[:, j * CB:(j + 1) * CB])
            p.inc(i1, vv)
        vv_scan = p.val(vv)

        def on_m3(mi):
            par = mi % 2
            if mi >= 2:
                p.wait(sy, x1v, p.val(x1v) - 2)
            p.inc(sy.dma_start(out=gx_g[:, par * H:(par + 1) * H],
                               in_=g_ap[mi * 128:(mi + 1) * 128, hs]), xg, 16)
            p.inc(sy.dma_start(out=gx_x[:, par * H:(par + 1) * H],
                               in_=xT_ap[mi * 128:(mi + 1) * 128, hs]), xg, 16)

        def evac3(mi, cb, pa, pb, hb=hb):
            n = pool.n_group - 1
            par2 = n % 2
            parm = mi % 2
            b0 = p.val(pool.bk)
            p.wait(v, pool.mm)
            p.wait(v, xg)
            if p.val(st[par2]) >= 16:
                p.wait(v, st[par2], p.val(st[par2]))
            i1 = v.scalar_tensor_tensor(
                t0f[:, par2 * 512:(par2 + 1) * 512], pa, bc(3, mi),
                gx_g[:, parm * H + cb * 512: parm * H + (cb + 1) * 512],
                ALU.add, ALU.mult)
            p.inc(i1, pool.bk)
            v.drain()
            i2 = v.tensor_add(ob[:, par2 * 512:(par2 + 1) * 512],
                              t0f[:, par2 * 512:(par2 + 1) * 512],
                              gx_x[:, parm * H + cb * 512: parm * H + (cb + 1) * 512])
            p.inc(i2, x1v)
            p.wait(sy, x1v)
            p.inc(sy.dma_start(out=x1_ap[mi * 128:(mi + 1) * 128,
                                         hb * H + cb * 512: hb * H + (cb + 1) * 512],
                               in_=ob[:, par2 * 512:(par2 + 1) * 512]), st[par2], 16)
            return p.val(pool.bk) - b0

        if io.get("dbg") is not None:
            dbg = io["dbg"]
            p.wait(sy, vv)
            p.inc(sy.dma_start(out=dbg["bc"], in_=bcols[:, :]), st[0], 16)
            for j in range(DT):
                p.inc(sy.dma_start(out=dbg["d"][j * 128:(j + 1) * 128, hs], in_=d_sb[:, j * H:(j + 1) * H]), st[0], 16)
                p.inc(sy.dma_start(out=dbg["u"][j * 128:(j + 1) * 128, hs], in_=u_sb[:, j * H:(j + 1) * H]), st[0], 16)
                p.inc(sy.dma_start(out=dbg["gt"][j * 128:(j + 1) * 128, hs], in_=gt_sb[:, j * H:(j + 1) * H]), st[0], 16)
                p.inc(sy.dma_start(out=dbg["y"][j * 128:(j + 1) * 128, hs], in_=y_sb[:, j * H:(j + 1) * H]), st[0], 16)
            p.wait(sy, st[0])
        mm_pass(p, pool, src_sb=y_sb, src_cols=H, n_k=16, n_m=16,
                Wimg=imgs["wout"], evac=evac3, on_m=on_m3, src_wait=(vv, vv_scan))
    p.wait(sy, st[0])
    p.wait(sy, st[1])


def build_s2(dbg=False):
    p = P()
    io = {}
    if dbg:
        io["dbg"] = {k: p.dram("dbg_" + k, [2048, S], BF16, "ExternalOutput")
                     for k in ["d", "u", "gt", "y"]}
        io["dbg"]["bc"] = p.dram("dbg_bc", [128, 80], F32, "ExternalOutput")
    xn = p.dram("xn_bf", [2048, S], BF16, "ExternalInput")
    xT = p.dram("xT", [2048, S], F32, "ExternalInput")
    gb = p.dram("g_bf", [2048, S], BF16, "ExternalInput")
    x1 = p.dram("x1", [2048, S], F32, "ExternalOutput")
    imgs = {k: p.dram(k + "_img", [2048, 2048], F32, "ExternalInput")
            for k in ["win", "wdec", "wgate", "wout"]}
    bias_aps = {k: p.dram(k + "_col", [128, 16], F32, "ExternalInput")
                for k in ["b_in", "b_dec", "b_gate", "b_out", "dec_bias"]}
    load_consts(p, io)
    emit_l0_attn(p, io, "s2", xn, xT, gb, x1, imgs, bias_aps)
    p.close()
    return p




def emit_moe(p, io, sfx, xn_ap, w_ap, wg_ap, wu_ap, wd_ap, g_ap):
    nc = p.nc
    g_, v, t, s_, sy = nc.gpsimd, nc.vector, nc.tensor, nc.scalar, nc.sync
    H = 1024
    DT = 16

    xn_sb = p.sbuf(sfx + "_xn", [128, DT * H], BF16)
    hp_sb = p.sbuf(sfx + "_hp", [128, DT * H], BF16)
    acc_sb = p.sbuf(sfx + "_acc", [128, DT * H], BF16)
    wb_sb = p.sbuf(sfx + "_wbc", [128, H], BF16)
    wrow = p.sbuf(sfx + "_wrow", [1, H], F32)
    tt = p.sbuf(sfx + "_tt", [128, 2 * 512], BF16)
    gbuf = p.sbuf(sfx + "_gb", [128, 2 * H], BF16)
    ps_w = p.psum([128, H])

    pool = Pool(p, sfx, max_nk=16, cbw=512, dual=True)
    ld = p.sem(sfx + "_ld")
    wld = p.sem(sfx + "_wld")
    wmm = p.sem(sfx + "_wmm")
    wbv = p.sem(sfx + "_wbv")
    sv = p.sem(sfx + "_sv")
    gv = p.sem(sfx + "_gv")
    st = p.sem(sfx + "_st")

    p.wait(g_, io["c_ld"])
    p.wait(t, io["c_ld"])
    ones_f32 = io["ones_f32_sb"]

    for hb in range(2):
        hs = slice(hb * H, (hb + 1) * H)
        if hb > 0:
            p.wait(sy, st)
        for j in range(DT):
            p.inc(sy.dma_start(out=xn_sb[:, j * H:(j + 1) * H],
                               in_=xn_ap[j * 128:(j + 1) * 128, hs]), ld, 16)
        ld_xn = p.val(ld)
        if hb > 0:
            p.wait(v, gv)

        for e in range(8):
            if e > 0 or hb > 0:
                p.wait(g_, wmm)
            p.inc(g_.dma_start(out=wrow[0:1, :], in_=w_ap[e:e + 1, hs]), wld, 16)
            p.wait(t, wld)
            if e > 0 or hb > 0:
                p.wait(t, wbv)
            t.matmul(ps_w[:, 0:512], ones_f32[0:1, :], wrow[0:1, 0:512], start=True, stop=True)
            i1 = t.matmul(ps_w[:, 512:1024], ones_f32[0:1, :], wrow[0:1, 512:1024], start=True, stop=True)
            p.inc(i1, wmm)
            p.wait(v, wmm)
            p.inc(v.tensor_copy(wb_sb[:, :], ps_w[:, :]), wbv)

            def evacA(mi, cb, pa, pb, e=e):
                n = pool.n_group - 1
                par = n % 2
                p.wait(s_, pool.mm)
                if p.val(pool.bk) >= 2:
                    p.wait(s_, pool.bk, p.val(pool.bk) - 1)
                i1 = s_.activation(tt[:, par * 512:(par + 1) * 512], pa, AF.Silu)
                p.inc(i1, sv)
                p.wait(v, sv)
                p.wait(v, wbv)
                dsl = hp_sb[:, mi * H + cb * 512: mi * H + (cb + 1) * 512]
                v.tensor_mul(dsl, tt[:, par * 512:(par + 1) * 512], pb)
                v.drain()
                i2 = v.tensor_mul(dsl, dsl,
                                  wb_sb[:, cb * 512:(cb + 1) * 512])
                p.inc(i2, pool.bk)
                return 1

            mm_pass(p, pool, src_sb=xn_sb, src_cols=H, n_k=16, n_m=16,
                    Wimg=wg_ap[e * 2048:(e + 1) * 2048, :],
                    W2img=wu_ap[e * 2048:(e + 1) * 2048, :],
                    evac=evacA, src_wait=(ld, ld_xn))
            hv_tgt = p.val(pool.bk)

            def evacB(mi, cb, pa, pb, e=e):
                p.wait(v, pool.mm)
                dsl = acc_sb[:, mi * H + cb * 512: mi * H + (cb + 1) * 512]
                if e == 0:
                    i1 = v.tensor_copy(dsl, pa)
                else:
                    i1 = v.tensor_add(dsl, dsl, pa)
                p.inc(i1, pool.bk)
                return 1

            mm_pass(p, pool, src_sb=hp_sb, src_cols=H, n_k=16, n_m=16,
                    Wimg=wd_ap[e * 2048:(e + 1) * 2048, :],
                    evac=evacB, src_wait=(pool.bk, hv_tgt))

        p.wait(s_, pool.bk)
        for j in range(DT):
            parj = j % 2
            if j >= 2:
                p.wait(s_, st, p.val(st))
            i1 = s_.activation(gbuf[:, parj * H:(parj + 1) * H],
                               acc_sb[:, j * H:(j + 1) * H], AF.Silu)
            p.inc(i1, gv)
            p.wait(sy, gv)
            p.inc(sy.dma_start(out=g_ap[j * 128:(j + 1) * 128, hs],
                               in_=gbuf[:, parj * H:(parj + 1) * H]), st, 16)
    p.wait(sy, st)


def build_s1():
    p = P()
    io = {}
    xn = p.dram("xn_bf", [2048, S], BF16, "ExternalInput")
    wf = p.dram("w_fm", [8, S], F32, "ExternalInput")
    wg = p.dram("wg_img", [8 * 2048, 2048], F32, "ExternalInput")
    wu = p.dram("wu_img", [8 * 2048, 2048], F32, "ExternalInput")
    wd = p.dram("wd_raw", [8 * 2048, 2048], F32, "ExternalInput")
    gb = p.dram("g_bf", [2048, S], BF16, "ExternalOutput")
    load_consts(p, io)
    emit_moe(p, io, "s1", xn, wf, wg, wu, wd, gb)
    p.close()
    return p



def emit_ffn(p, io, sfx, xn_ap, w0_ap, w1_ap, b0_ap, b1_ap, g2_ap):
    nc = p.nc
    g_, v, t, s_, sy = nc.gpsimd, nc.vector, nc.tensor, nc.scalar, nc.sync
    Q = 1024
    DT = 16

    xn_sb = p.sbuf(sfx + "_xn", [128, DT * Q], BF16)
    dff_sb = p.sbuf(sfx + "_dff", [128, 64 * Q], BF16)
    g2buf = p.sbuf(sfx + "_g2b", [128, 2 * 512], BF16)
    bcols = p.sbuf(sfx + "_bc", [128, 64 + 16], F32)

    pool = Pool(p, sfx, max_nk=64, cbw=512, dual=False, depth=2)
    ld = p.sem(sfx + "_ld")
    bld = p.sem(sfx + "_bld")
    st = p.sem(sfx + "_st")
    stp = [p.sem(sfx + "_stp0"), p.sem(sfx + "_stp1")]

    p.wait(g_, io["c_ld"])
    p.inc(g_.dma_start(out=bcols[:, 0:64], in_=b0_ap), bld, 16)
    p.inc(g_.dma_start(out=bcols[:, 64:80], in_=b1_ap), bld, 16)
    p.wait(s_, bld)

    for q in range(S // Q):
        qs = slice(q * Q, (q + 1) * Q)
        if q > 0:
            p.wait(sy, stp[0])
            p.wait(sy, stp[1])
        for j in range(DT):
            p.inc(sy.dma_start(out=xn_sb[:, j * Q:(j + 1) * Q],
                               in_=xn_ap[j * 128:(j + 1) * 128, qs]), ld, 16)
        ld_xn = p.val(ld)

        def evac0(mi, cb, pa, pb):
            p.wait(s_, pool.mm)
            i1 = s_.activation(dff_sb[:, mi * Q + cb * 512: mi * Q + (cb + 1) * 512],
                               pa, AF.Gelu, bias=bcols[:, mi:mi + 1])
            p.inc(i1, pool.bk)
            return 1

        mm_pass(p, pool, src_sb=xn_sb, src_cols=Q, n_k=16, n_m=64,
                Wimg=w0_ap, evac=evac0, src_wait=(ld, ld_xn))

        def evac1(mi, cb, pa, pb, q=q):
            n = pool.n_group - 1
            par = n % 2
            p.wait(s_, pool.mm)
            if p.val(stp[par]) >= 16:
                p.wait(s_, stp[par], p.val(stp[par]))
            i1 = s_.activation(g2buf[:, par * 512:(par + 1) * 512], pa, AF.Silu,
                               bias=bcols[:, 64 + mi:64 + mi + 1])
            p.inc(i1, pool.bk)
            p.wait(sy, pool.bk)
            i2 = sy.dma_start(out=g2_ap[mi * 128:(mi + 1) * 128,
                                        q * Q + cb * 512: q * Q + (cb + 1) * 512],
                              in_=g2buf[:, par * 512:(par + 1) * 512])
            p.inc(i2, stp[par], 16)
            return 1

        mm_pass(p, pool, src_sb=dff_sb, src_cols=Q, n_k=64, n_m=16,
                Wimg=w1_ap, evac=evac1, src_wait=(pool.bk, p.val(pool.bk)))
    p.wait(sy, stp[0])
    p.wait(sy, stp[1])


def build_s3():
    p = P()
    io = {}
    x1 = p.dram("x1", [2048, S], F32, "ExternalInput")
    xn2 = p.dram("xn2_bf", [2048, S], BF16, "ExternalOutput")
    load_consts(p, io)
    emit_ln_router(p, io, "s3", x1, xn2, router=False)
    p.close()
    return p


def build_s4():
    p = P()
    io = {}
    xn2 = p.dram("xn2_bf", [2048, S], BF16, "ExternalInput")
    w0 = p.dram("w0_img", [8192, 2048], F32, "ExternalInput")
    w1 = p.dram("w1_img", [2048, 8192], F32, "ExternalInput")
    b0 = p.dram("b0_col", [128, 64], F32, "ExternalInput")
    b1 = p.dram("b1_col", [128, 16], F32, "ExternalInput")
    g2 = p.dram("g2_bf", [2048, S], BF16, "ExternalOutput")
    load_consts(p, io)
    emit_ffn(p, io, "s4", xn2, w0, w1, b0, b1, g2)
    p.close()
    return p




def emit_zxbcdt(p, io, sfx, xn_ap, win_ap, taps_ap, bconv_ap, dtb_ap, alog_ap,
                z_ap, xbc_ap, a_ap):
    nc = p.nc
    g_, v, t, s_, sy = nc.gpsimd, nc.vector, nc.tensor, nc.scalar, nc.sync
    H = 1024
    DT = 16

    xn_sb = p.sbuf(sfx + "_xn", [128, DT * H], BF16)
    zbuf = p.sbuf(sfx + "_zb", [128, 2 * 512], BF16)
    ext = p.sbuf(sfx + "_ext", [128, H + 3], F32)
    cbuf = p.sbuf(sfx + "_cb", [128, 4 * 512], F32)
    xbcb = p.sbuf(sfx + "_xbcb", [128, 2 * 512], BF16)
    hist = p.sbuf(sfx + "_hist", [128, 34 * 3], F32)
    taps = p.sbuf(sfx + "_taps", [128, 34 * 4], F32)
    bconv = p.sbuf(sfx + "_bcv", [128, 34], F32)
    dtb = p.sbuf(sfx + "_dtb", [64, 1], F32)
    alog = p.sbuf(sfx + "_alog", [64, 1], F32)
    nega = p.sbuf(sfx + "_nega", [64, 2], F32)
    dts = p.sbuf(sfx + "_dts", [64, 6 * 512], F32)
    abuf = p.sbuf(sfx + "_ab", [64, 2 * 512], F32)

    pool = Pool(p, sfx, max_nk=16, cbw=512, dual=False)
    ld = p.sem(sfx + "_ld")
    bld = p.sem(sfx + "_bld")
    st = p.sem(sfx + "_st")
    cv = p.sem(sfx + "_cv")
    sc2 = p.sem(sfx + "_sc2")

    p.wait(g_, io["c_ld"])
    for ap, sb in [(taps_ap, taps), (bconv_ap, bconv), (dtb_ap, dtb), (alog_ap, alog)]:
        p.inc(g_.dma_start(out=sb[:, :] if sb is not dtb and sb is not alog else sb[0:64, :], in_=ap), bld, 16)
    p.wait(s_, bld)
    i0 = s_.activation(nega[0:64, 0:1], alog[0:64, 0:1], AF.Exp)
    p.inc(i0, sc2)
    p.wait(v, sc2)
    i1 = v.tensor_scalar_mul(nega[0:64, 1:2], nega[0:64, 0:1], -1.0)
    p.inc(i1, cv)
    p.wait(v, bld)
    v.memset(hist[:, :], 0.0)
    i2 = v.memset(ext[:, :], 0.0)
    p.inc(i2, cv)

    for hb in range(2):
        hs = slice(hb * H, (hb + 1) * H)
        if hb > 0:
            p.wait(sy, st)
        for j in range(DT):
            p.inc(sy.dma_start(out=xn_sb[:, j * H:(j + 1) * H],
                               in_=xn_ap[j * 128:(j + 1) * 128, hs]), ld, 16)
        ld_xn = p.val(ld)

        def evac(mi, cb, pa, pb, hb=hb, hs=hs):
            n = pool.n_group - 1
            par = n % 2
            if mi < 32:
                p.wait(s_, pool.mm)
                p.wait(s_, st, p.val(st))
                i1 = s_.activation(zbuf[:, par * 512:(par + 1) * 512], pa, AF.Silu)
                p.inc(i1, pool.bk)
                p.wait(sy, pool.bk)
                p.inc(sy.dma_start(out=z_ap[mi * 128:(mi + 1) * 128,
                                            hb * H + cb * 512: hb * H + (cb + 1) * 512],
                                   in_=zbuf[:, par * 512:(par + 1) * 512]), st, 16)
                return 1
            if mi < 66:
                ci = mi - 32
                p.wait(s_, pool.mm)
                if cb == 0:
                    p.wait(s_, cv)
                i1 = s_.activation(ext[:, 3 + cb * 512: 3 + (cb + 1) * 512], pa, AF.Identity)
                p.inc(i1, pool.bk)
                p.wait(v, pool.bk)
                c0 = cb * 512
                t1 = cbuf[:, (par * 2) * 512:(par * 2 + 1) * 512]
                t2 = cbuf[:, (par * 2 + 1) * 512:(par * 2 + 2) * 512]
                if cb == 0 and hb > 0:
                    v.tensor_copy(ext[:, 0:3], hist[:, ci * 3:(ci + 1) * 3])
                    v.drain()
                v.tensor_scalar_mul(t1, ext[:, c0:c0 + 512], taps[:, ci * 4: ci * 4 + 1])
                v.drain()
                v.scalar_tensor_tensor(t2, ext[:, c0 + 1:c0 + 513], taps[:, ci * 4 + 1: ci * 4 + 2],
                                       t1, ALU.mult, ALU.add)
                v.drain()
                v.scalar_tensor_tensor(t1, ext[:, c0 + 2:c0 + 514], taps[:, ci * 4 + 2: ci * 4 + 3],
                                       t2, ALU.mult, ALU.add)
                v.drain()
                i2 = v.scalar_tensor_tensor(t2, ext[:, c0 + 3:c0 + 515], taps[:, ci * 4 + 3: ci * 4 + 4],
                                            t1, ALU.mult, ALU.add)
                if cb == 1:
                    v.drain()
                    i2 = v.tensor_copy(hist[:, ci * 3:(ci + 1) * 3], ext[:, H:H + 3])
                p.inc(i2, cv)
                p.wait(s_, cv)
                p.wait(s_, st, p.val(st))
                i3 = s_.activation(xbcb[:, par * 512:(par + 1) * 512], t2, AF.Silu,
                                   bias=bconv[:, ci:ci + 1])
                p.inc(i3, pool.bk)
                p.wait(sy, pool.bk)
                p.inc(sy.dma_start(out=xbc_ap[ci * 128:(ci + 1) * 128,
                                              hb * H + cb * 512: hb * H + (cb + 1) * 512],
                                   in_=xbcb[:, par * 512:(par + 1) * 512]), st, 16)
                return 2
            p.wait(s_, pool.mm)
            p.wait(s_, cv, p.val(cv))
            xr = dts[0:64, 0:512]
            a1 = dts[0:64, 512:1024]
            i1 = s_.activation(xr, pa[0:64, :], AF.Identity, bias=dtb[0:64, 0:1])
            s_.drain()
            s_.activation(a1, xr, AF.Abs)
            s_.drain()
            s_.activation(dts[0:64, 1024:1536], a1, AF.Exp, scale=-1.0)
            s_.drain()
            i2 = s_.activation(dts[0:64, 1536:2048], dts[0:64, 1024:1536], AF.Ln, bias=1.0)
            p.inc(i2, pool.bk)
            p.wait(v, pool.bk)
            v.tensor_scalar_max(dts[0:64, 2048:2560], xr, 0.0)
            v.drain()
            v.tensor_add(dts[0:64, 2560:3072], dts[0:64, 2048:2560], dts[0:64, 1536:2048])
            v.drain()
            pard = pool.n_group % 2
            p.wait(v, st, p.val(st))
            i3 = v.tensor_scalar_mul(abuf[0:64, pard * 512:(pard + 1) * 512],
                                     dts[0:64, 2560:3072], nega[0:64, 1:2])
            p.inc(i3, cv)
            p.wait(sy, cv)
            p.inc(sy.dma_start(out=a_ap[0:64, hb * H + cb * 512: hb * H + (cb + 1) * 512],
                               in_=abuf[0:64, pard * 512:(pard + 1) * 512]), st, 16)
            return 1

        mm_pass(p, pool, src_sb=xn_sb, src_cols=H, n_k=16, n_m=67,
                Wimg=win_ap, evac=evac, src_wait=(ld, ld_xn))
    p.wait(sy, st)


def build_s5():
    p = P()
    io = {}
    xn2 = p.dram("xn2_bf", [2048, S], BF16, "ExternalInput")
    win = p.dram("win_img", [67 * 128, 2048], F32, "ExternalInput")
    taps = p.dram("taps", [128, 34 * 4], F32, "ExternalInput")
    bconv = p.dram("bconv", [128, 34], F32, "ExternalInput")
    dtb = p.dram("dtb", [64, 1], F32, "ExternalInput")
    alog = p.dram("alog", [64, 1], F32, "ExternalInput")
    z = p.dram("z_silu", [4096, S], BF16, "ExternalOutput")
    xbc = p.dram("xBC_bf", [34 * 128, S], BF16, "ExternalOutput")
    a = p.dram("A_arr", [64, S], F32, "ExternalOutput")
    load_consts(p, io)
    emit_zxbcdt(p, io, "s5", xn2, win, taps, bconv, dtb, alog, z, xbc, a)
    p.close()
    return p




def emit_ssd(p, io, sfx, xbc_ap, z_ap, a_ap, dp_ap, wn_ap, yn_ap):
    nc = p.nc
    g_, v, t, s_, sy = nc.gpsimd, nc.vector, nc.tensor, nc.scalar, nc.sync
    NCH = 16

    xbc_sb2 = p.sbuf(sfx + "_xbc", [128, 2 * 34 * 128], BF16)
    z_sb2 = p.sbuf(sfx + "_z", [128, 2 * 32 * 128], BF16)
    a_sb2 = p.sbuf(sfx + "_a", [64, 2 * 128], F32)
    cs = p.sbuf(sfx + "_cs", [64, 128], F32)
    z64 = p.sbuf(sfx + "_z64", [64, 128], F32)
    a64 = p.sbuf(sfx + "_a64", [64, 4 * 128], BF16)
    cm = p.sbuf(sfx + "_cm", [64, 128], F32)
    gsel = p.sbuf(sfx + "_gsel", [64, 4096], BF16)
    dec_T = p.sbuf(sfx + "_decT", [128, 64], BF16)
    stage1 = p.sbuf(sfx + "_stg1", [4, 8192], BF16)
    stage2 = p.sbuf(sfx + "_stg2", [4, 8192], BF16)
    X_T = p.sbuf(sfx + "_XT", [128, 4096], BF16)
    Xw_T = p.sbuf(sfx + "_XwT", [128, 4096], BF16)
    B_T = p.sbuf(sfx + "_BT", [128, 128], BF16)
    G_m = p.sbuf(sfx + "_Gm", [128, 128], BF16)
    L_sb = p.sbuf(sfx + "_L", [128, 2 * 1024], BF16)
    M_sb = p.sbuf(sfx + "_M", [128, 8192], BF16)
    sdo = p.sbuf(sfx + "_sdo", [128, 4096], BF16)
    y3 = p.sbuf(sfx + "_y3", [128, 4096], BF16)
    sq = p.sbuf(sfx + "_sq", [128, 4096], BF16)
    F_sb = sq
    ynb = p.sbuf(sfx + "_ynb", [128, 4096], BF16)
    tf = p.sbuf(sfx + "_tf", [128, 2 * 512], F32)
    run = p.sbuf(sfx + "_run", [128, 4096], F32)
    run_bf = p.sbuf(sfx + "_runbf", [128, 4096], BF16)
    dg = p.sbuf(sfx + "_dg", [128, 32 * 128], BF16)
    dp_sb = p.sbuf(sfx + "_dp", [128, 32], F32)
    wn_sb = p.sbuf(sfx + "_wn", [128, 32], F32)
    rowbuf = p.sbuf(sfx + "_row", [1, 4 * 128], F32)
    inv_b = p.sbuf(sfx + "_invb", [128, 128], BF16)
    zeros64 = p.sbuf(sfx + "_z0", [64, 128], F32)

    ps1 = p.psum([128, 1024])
    ps2 = p.psum([128, 1024])
    ps3 = p.psum([128, 512])
    ps4 = p.psum([128, 512])
    psT = p.psum([128, 1024], BF16)
    psR = p.psum([128, 128])

    ld = [p.sem(sfx + "_ld0"), p.sem(sfx + "_ld1")]
    cdone = p.sem(sfx + "_cdone")
    bld = p.sem(sfx + "_bld")
    tv = p.sem(sfx + "_tv")
    vv = p.sem(sfx + "_vv")
    sc = p.sem(sfx + "_sc")
    gp = p.sem(sfx + "_gp")
    sg = p.sem(sfx + "_sg")
    st = p.sem(sfx + "_st")

    ones_bf = io["ones_bf_sb"]
    ones_f32 = io["ones_f32_sb"]
    ident_bf = io["ident_bf_sb"]
    ident_f32 = io["ident_f32_sb"]
    triu01 = io["triu01_sb"]
    sel = p.sbuf(sfx + "_sel", [64, 32 * 128], BF16)
    sel64 = p.sbuf(sfx + "_sel64", [64, 4096], BF16)

    p.wait(g_, io["c_ld"])
    p.wait(t, io["c_ld"])
    p.inc(g_.dma_start(out=sel[:, :], in_=io["sel_ap"]), bld, 16)
    p.inc(g_.dma_start(out=sel64[:, :], in_=io["sel64_ap"]), bld, 16)
    p.inc(g_.dma_start(out=dp_sb[:, :], in_=dp_ap), bld, 16)
    p.inc(g_.dma_start(out=wn_sb[:, :], in_=wn_ap), bld, 16)
    p.wait(v, bld)
    p.wait(t, bld)
    p.wait(v, io["c_ld"])
    v.memset(run[:, :], 0.0)
    v.memset(zeros64[:, :], 0.0)
    v.memset(stage1[0:4, :], 1.0)
    v.memset(stage2[0:4, :], 1.0)
    v.drain()
    for j in range(32):
        i1 = v.tensor_scalar_mul(dg[:, j * 128:(j + 1) * 128], ident_bf[:, :],
                                 dp_sb[:, j:j + 1])
    p.inc(i1, vv)

    def issue_loads(c):
        par = c % 2
        csl = slice(c * 128, (c + 1) * 128)
        if c >= 2:
            p.wait(sy, cdone, c - 1)
        for j0, jn in [(0, 17), (17, 17)]:
            p.inc(sy.dma_start(
                out=xbc_sb2[:, (par * 34 + j0) * 128:(par * 34 + j0 + jn) * 128]
                    .rearrange("p (j c) -> p j c", j=jn),
                in_=xbc_ap[j0 * 128:(j0 + jn) * 128, csl]
                    .rearrange("(j p) c -> p j c", p=128)), ld[par], 16)
        for j0 in (0, 16):
            p.inc(sy.dma_start(
                out=z_sb2[:, (par * 32 + j0) * 128:(par * 32 + j0 + 16) * 128]
                    .rearrange("p (j c) -> p j c", j=16),
                in_=z_ap[j0 * 128:(j0 + 16) * 128, csl]
                    .rearrange("(j p) c -> p j c", p=128)), ld[par], 16)
        p.inc(sy.dma_start(out=a_sb2[0:64, par * 128:(par + 1) * 128],
                           in_=a_ap[0:64, csl]), ld[par], 16)

    issue_loads(0)
    xw_vv = []
    psr_sc = []
    for c in range(NCH):
        parc_in = c % 2
        if c + 1 < NCH:
            issue_loads(c + 1)
        xbc_sb = xbc_sb2[:, parc_in * 34 * 128:(parc_in + 1) * 34 * 128]
        z_sb = z_sb2[:, parc_in * 32 * 128:(parc_in + 1) * 32 * 128]
        a_sb = a_sb2[:, parc_in * 128:(parc_in + 1) * 128]
        csl = slice(c * 128, (c + 1) * 128)
        ld_c = p.val(ld[parc_in])

        p.wait(v, ld[parc_in], ld_c)
        v.drain()
        v.tensor_tensor_scan(cs[0:64, :], a_sb[0:64, :], zeros64[0:64, :], 0.0,
                             ALU.add, ALU.add)
        v.drain()
        v.tensor_copy(a64[0:64, 0:128], cs[0:64, :])
        v.drain()
        v.tensor_sub(a64[0:64, 128:256], cs[0:64, :], a64[0:64, 0:128])
        v.drain()
        v.tensor_scalar_mul(a64[0:64, 256:384], a64[0:64, 0:128], -1.0)
        v.tensor_scalar_mul(a64[0:64, 384:512], a64[0:64, 128:256], -1.0)
        v.drain()
        i1 = v.tensor_single_scalar(cm[0:64, :], cs[0:64, :], cs[0:64, 127:128],
                                    ALU.subtract)
        p.inc(i1, vv)
        p.wait(s_, vv)
        i2 = s_.activation(z64[0:64, 0:1], cs[0:64, 127:128], AF.Exp)
        p.inc(i2, sc)
        p.wait(v, sc)
        i3 = v.tensor_scalar_mul(gsel[0:64, :], sel64[0:64, :], z64[0:64, 0:1])
        p.inc(i3, vv)
        p.wait(g_, vv)
        p.inc(g_.dma_start(out=stage2[2:3, :], in_=a64[0:64, 0:128]), sg, 16)
        p.inc(g_.dma_start(out=stage2[3:4, :], in_=a64[0:64, 128:256]), sg, 16)
        p.inc(g_.dma_start(out=stage1[0:1, :], in_=a64[0:64, 256:384]), sg, 16)
        p.inc(g_.dma_start(out=stage1[1:2, :], in_=a64[0:64, 384:512]), sg, 16)

        p.wait(t, ld[parc_in], ld_c)
        if c > 0:
            p.wait(t, vv, xw_vv[-1])
        for r in range(4):
            for i in range(8):
                i6 = t.transpose(psT[:, i * 128:(i + 1) * 128],
                                 xbc_sb[:, (r * 8 + i) * 128:(r * 8 + i + 1) * 128],
                                 ident_bf[:, :])
            p.inc(i6, tv)
            p.wait(v, tv)
            i7 = v.tensor_copy(X_T[:, r * 1024:(r + 1) * 1024], psT[:, :])
            p.inc(i7, vv)
            p.wait(t, vv)
        i8 = t.transpose(psT[:, 0:128], xbc_sb[:, 32 * 128:33 * 128], ident_bf[:, :])
        p.inc(i8, tv)
        p.wait(v, tv)
        i9 = v.tensor_copy(B_T[:, :], psT[:, 0:128])
        p.inc(i9, vv)

        p.wait(t, vv)
        if c > 0:
            p.wait(t, sc, psr_sc[-1])
        i4 = t.transpose(psR[:, 0:64], cm[0:64, :], ident_f32[0:64, 0:64])
        p.inc(i4, tv)
        p.wait(s_, tv)
        i5 = s_.activation(dec_T[:, 0:64], psR[:, 0:64], AF.Exp, scale=-1.0)
        p.inc(i5, sc)
        dec_sc = p.val(sc)

        p.wait(v, sc, dec_sc)
        v.drain()
        i10 = v.tensor_tensor(Xw_T[:, :].rearrange("p (h q) -> p h q", h=64),
                              X_T[:, :].rearrange("p (h q) -> p h q", h=64),
                              dec_T[:, 0:64].unsqueeze(2).broadcast_to([128, 64, 64]),
                              ALU.mult)
        p.inc(i10, vv)
        xw_vv.append(p.val(vv))

        p.wait(t, sc, dec_sc)
        i11 = t.matmul(psR[:, :], xbc_sb[:, 32 * 128:33 * 128],
                       xbc_sb[:, 33 * 128:34 * 128], start=True, stop=True)
        p.inc(i11, tv)
        p.wait(v, tv)
        i12 = v.tensor_tensor(G_m[:, :], psR[:, :], triu01[:, :], ALU.mult)
        p.inc(i12, vv)
        psr_sc.append(p.val(sc))

        p.wait(t, sg)
        p.wait(t, vv)
        p.wait(t, sc)
        sdo_sc = []
        for r in range(4):
            pp = (ps1, ps2)[r % 2]
            if r >= 2:
                p.wait(t, sc, sdo_sc[r - 2])
            for i in range(8):
                j = r * 8 + i
                t.matmul(pp[:, i * 128:(i + 1) * 128], sel[0:64, j * 128:(j + 1) * 128],
                         a64[0:64, 0:128], start=True, stop=False)
                i13 = t.matmul(pp[:, i * 128:(i + 1) * 128], sel[0:64, j * 128:(j + 1) * 128],
                               a64[0:64, 128:256], start=False, stop=True)
            p.inc(i13, tv)
            p.wait(s_, tv)
            i14 = s_.activation(sdo[:, r * 1024:(r + 1) * 1024], pp[:, :], AF.Exp)
            p.inc(i14, sc)
            sdo_sc.append(p.val(sc))

        p.wait(v, sc)
        d_sc = []
        mv_vv = []
        for r in range(8):
            pp = (ps1, ps2)[r % 2]
            par = r % 2
            p.wait(t, sc, sdo_sc[2 + r] if r < 2 else d_sc[r - 2])
            for i in range(8):
                h = r * 8 + i
                i15 = t.matmul(pp[:, i * 128:(i + 1) * 128],
                               stage1[0:4, h * 128:(h + 1) * 128],
                               stage2[0:4, h * 128:(h + 1) * 128], start=True, stop=True)
            p.inc(i15, tv)
            p.wait(s_, tv)
            if r >= 2:
                p.wait(s_, vv, mv_vv[r - 2])
            i16 = s_.activation(L_sb[:, par * 1024:(par + 1) * 1024], pp[:, :], AF.Exp)
            p.inc(i16, sc)
            d_sc.append(p.val(sc))
            p.wait(v, sc, d_sc[-1])
            i18 = v.scalar_tensor_tensor(
                M_sb[:, r * 1024:(r + 1) * 1024].rearrange("p (i l) -> p i l", i=8),
                L_sb[:, par * 1024:(par + 1) * 1024].rearrange("p (i l) -> p i l", i=8),
                1e30,
                G_m[:, :].unsqueeze(1).broadcast_to([128, 8, 128]),
                ALU.min, ALU.mult)
            p.inc(i18, vv)
            mv_vv.append(p.val(vv))

        p.wait(v, vv)
        v.drain()
        i19 = v.tensor_copy(run_bf[:, :], run[:, :])
        p.inc(i19, vv)
        p.wait(t, vv)
        p.wait(t, sc)
        yv = []
        for g4 in range(8):
            par = g4 % 2
            pa_y = (ps3, ps1)[par]
            pb_y = (ps4, ps2)[par]
            if g4 >= 2:
                p.wait(t, vv, yv[g4 - 2])
            for j in range(4):
                jt = g4 * 4 + j
                t.matmul(pa_y[:, j * 128:(j + 1) * 128], run_bf[:, jt * 128:(jt + 1) * 128],
                         xbc_sb[:, 33 * 128:34 * 128], start=True, stop=True)
            for j in range(4):
                jt = g4 * 4 + j
                t.matmul(pb_y[:, j * 128:(j + 1) * 128], dg[:, jt * 128:(jt + 1) * 128],
                         xbc_sb[:, jt * 128:(jt + 1) * 128], start=True, stop=False)
                h0 = jt * 2
                t.matmul(pb_y[0:64, j * 128:(j + 1) * 128], X_T[:, h0 * 64:(h0 + 1) * 64],
                         M_sb[:, h0 * 128:(h0 + 1) * 128], start=False, stop=False)
                i20 = t.matmul(pb_y[64:128, j * 128:(j + 1) * 128], X_T[:, (h0 + 1) * 64:(h0 + 2) * 64],
                               M_sb[:, (h0 + 1) * 128:(h0 + 2) * 128], start=False, stop=True)
            p.inc(i20, tv)
            p.wait(v, tv)
            v.tensor_tensor(tf[:, par * 512:(par + 1) * 512], pa_y[:, 0:512],
                            sdo[:, g4 * 512:(g4 + 1) * 512], ALU.mult)
            v.drain()
            v.tensor_add(tf[:, par * 512:(par + 1) * 512], tf[:, par * 512:(par + 1) * 512],
                         pb_y[:, 0:512])
            v.drain()
            i21 = v.tensor_mul(y3[:, g4 * 512:(g4 + 1) * 512],
                               tf[:, par * 512:(par + 1) * 512],
                               z_sb[:, g4 * 512:(g4 + 1) * 512])
            p.inc(i21, vv)
            yv.append(p.val(vv))

        for q in range(4):
            pp_s = (ps1, ps2)[q % 2]
            pp_g = (ps2, ps1)[q % 2]
            p.wait(t, vv)
            for hf in range(2):
                t.matmul(pp_s[:, hf * 512:(hf + 1) * 512], B_T[:, :],
                         Xw_T[:, q * 1024 + hf * 512: q * 1024 + (hf + 1) * 512],
                         start=True, stop=True)
                i22 = t.matmul(pp_g[:, hf * 512:(hf + 1) * 512], ones_bf[0:64, :],
                               gsel[0:64, q * 1024 + hf * 512: q * 1024 + (hf + 1) * 512],
                               start=True, stop=True)
            p.inc(i22, tv)
            p.wait(v, tv)
            v.drain()
            v.tensor_tensor(run[:, q * 1024:(q + 1) * 1024], run[:, q * 1024:(q + 1) * 1024],
                            pp_g[:, :], ALU.mult)
            v.drain()
            i23 = v.tensor_add(run[:, q * 1024:(q + 1) * 1024], run[:, q * 1024:(q + 1) * 1024],
                               pp_s[:, :])
            p.inc(i23, vv)

        v.drain()
        i24 = v.tensor_mul(sq[:, :], y3[:, :], y3[:, :])
        p.inc(i24, vv)
        p.wait(t, vv)
        for j in range(32):
            i25 = t.matmul(psR[0:1, :], ones_bf[:, 0:1], sq[:, j * 128:(j + 1) * 128],
                           start=(j == 0), stop=(j == 31))
        p.inc(i25, tv)
        p.wait(v, tv)
        v.tensor_scalar(rowbuf[0:1, 0:128], psR[0:1, :], 1.0 / 4096.0, 1e-5,
                        ALU.mult, ALU.add)
        v.drain()
        i26 = v.reciprocal(rowbuf[0:1, 128:256], rowbuf[0:1, 0:128])
        p.inc(i26, vv)
        p.wait(s_, vv)
        s_.activation(rowbuf[0:1, 256:384], rowbuf[0:1, 128:256], AF.Ln)
        s_.drain()
        i27 = s_.activation(rowbuf[0:1, 384:512], rowbuf[0:1, 256:384], AF.Exp, scale=0.5)
        p.inc(i27, sc)
        p.wait(t, sc)
        i28 = t.matmul(psR[:, :], ones_f32[0:1, :], rowbuf[0:1, 384:512], start=True, stop=True)
        p.inc(i28, tv)
        p.wait(v, tv)
        v.tensor_copy(inv_b[:, :], psR[:, :])
        v.drain()
        v.tensor_tensor(F_sb[:, :].rearrange("p (j l) -> p j l", j=32),
                        wn_sb[:, :].unsqueeze(2).broadcast_to([128, 32, 128]),
                        inv_b[:, :].unsqueeze(1).broadcast_to([128, 32, 128]),
                        ALU.mult)
        v.drain()
        if c >= 1:
            p.wait(v, st, p.val(st))
        i29 = v.tensor_mul(ynb[:, :], y3[:, :], F_sb[:, :])
        p.inc(i29, vv)
        p.wait(s_, vv)
        for j0 in (0, 16):
            i30 = s_.dma_start(out=yn_ap[j0 * 128:(j0 + 16) * 128, csl]
                               .rearrange("(j p) c -> p j c", p=128),
                               in_=ynb[:, j0 * 128:(j0 + 16) * 128]
                               .rearrange("p (j c) -> p j c", j=16))
            p.inc(i30, st, 16)
        p.inc(sy.nop(), cdone, 1) if False else p.inc(s_.nop(), cdone, 1)
    p.wait(sy, st)


def build_s6():
    p = P()
    io = {}
    xbc = p.dram("xBC_bf", [34 * 128, S], BF16, "ExternalInput")
    z = p.dram("z_silu", [4096, S], BF16, "ExternalInput")
    a = p.dram("A_arr", [64, S], F32, "ExternalInput")
    dp = p.dram("dp_hp", [128, 32], F32, "ExternalInput")
    wn = p.dram("wn_hp", [128, 32], F32, "ExternalInput")
    yn = p.dram("y_n", [4096, S], BF16, "ExternalOutput")
    load_consts(p, io)
    emit_ssd(p, io, "s6", xbc, z, a, dp, wn, yn)
    p.close()
    return p



def emit_l1_out(p, io, sfx, yn_ap, wout_ap, g2_ap, x1_ap, out_ap):
    nc = p.nc
    g_, v, t, s_, sy = nc.gpsimd, nc.vector, nc.tensor, nc.scalar, nc.sync

    yn_sb = p.sbuf(sfx + "_yn", [128, 32 * 2048], BF16)
    gx_g = p.sbuf(sfx + "_gxg", [128, 2 * 2048], BF16)
    gx_x = p.sbuf(sfx + "_gxx", [128, 2 * 2048], F32)
    t0f = p.sbuf(sfx + "_t0f", [128, 2 * 512], F32)
    ob = p.sbuf(sfx + "_ob", [128, 2 * 512], F32)

    pool = Pool(p, sfx, max_nk=32, cbw=512, dual=False)
    ld = p.sem(sfx + "_ld")
    xg = p.sem(sfx + "_xg")
    x1v = p.sem(sfx + "_x1v")
    st = [p.sem(sfx + "_st0"), p.sem(sfx + "_st1")]

    p.wait(g_, io["c_ld"])
    p.wait(t, io["c_ld"])
    for j in range(32):
        p.inc(sy.dma_start(out=yn_sb[:, j * 2048:(j + 1) * 2048],
                           in_=yn_ap[j * 128:(j + 1) * 128, :]), ld, 16)
    ld_yn = p.val(ld)

    def on_m(mi):
        par = mi % 2
        if mi >= 2:
            p.wait(sy, x1v, p.val(x1v) - 4)
        p.inc(sy.dma_start(out=gx_g[:, par * 2048:(par + 1) * 2048],
                           in_=g2_ap[mi * 128:(mi + 1) * 128, :]), xg, 16)
        p.inc(sy.dma_start(out=gx_x[:, par * 2048:(par + 1) * 2048],
                           in_=x1_ap[mi * 128:(mi + 1) * 128, :]), xg, 16)

    def evac(mi, cb, pa, pb):
        n = pool.n_group - 1
        par2 = n % 2
        parm = mi % 2
        p.wait(v, pool.mm)
        p.wait(v, xg)
        if p.val(st[par2]) >= 16:
            p.wait(v, st[par2], p.val(st[par2]))
        i1 = v.tensor_mul(t0f[:, par2 * 512:(par2 + 1) * 512], pa,
                          gx_g[:, parm * 2048 + cb * 512: parm * 2048 + (cb + 1) * 512])
        p.inc(i1, pool.bk)
        v.drain()
        i2 = v.tensor_add(ob[:, par2 * 512:(par2 + 1) * 512],
                          t0f[:, par2 * 512:(par2 + 1) * 512],
                          gx_x[:, parm * 2048 + cb * 512: parm * 2048 + (cb + 1) * 512])
        p.inc(i2, x1v)
        p.wait(sy, x1v)
        p.inc(sy.dma_start(out=out_ap[mi * 128:(mi + 1) * 128, cb * 512:(cb + 1) * 512],
                           in_=ob[:, par2 * 512:(par2 + 1) * 512]), st[par2], 16)
        return 1

    mm_pass(p, pool, src_sb=yn_sb, src_cols=2048, n_k=32, n_m=16,
            Wimg=wout_ap, evac=evac, on_m=on_m, src_wait=(ld, ld_yn))
    p.wait(sy, st[0])
    p.wait(sy, st[1])


def build_s7():
    p = P()
    io = {}
    yn = p.dram("y_n", [4096, S], BF16, "ExternalInput")
    wout = p.dram("wout1_img", [2048, 4096], F32, "ExternalInput")
    g2 = p.dram("g2_bf", [2048, S], BF16, "ExternalInput")
    x1 = p.dram("x1", [2048, S], F32, "ExternalInput")
    out = p.dram("outT", [2048, S], F32, "ExternalOutput")
    load_consts(p, io)
    emit_l1_out(p, io, "s7", yn, wout, g2, x1, out)
    p.close()
    return p



_PROGS = {}


def _progs():
    if not _PROGS:
        _PROGS["s0"] = build_s0()
        _PROGS["s1"] = build_s1()
        _PROGS["s2"] = build_s2()
        _PROGS["s3"] = build_s3()
        _PROGS["s4"] = build_s4()
        _PROGS["s5"] = build_s5()
        _PROGS["s6"] = build_s6()
        _PROGS["s7"] = build_s7()
    return _PROGS


def _host_images(inp):
    im = {}
    im.update(consts_host())
    im["wr_img"] = host_wr_img(inp["L0_ffn_Wr"])
    im["win_img"] = _img_lhsT(inp["L0_attn_W_in"]).reshape(2048, 2048)
    im["wdec_img"] = _img_lhsT(inp["L0_attn_W_decay"]).reshape(2048, 2048)
    im["wgate_img"] = _img_lhsT(inp["L0_attn_W_gate"]).reshape(2048, 2048)
    im["wout_img"] = _img_lhsT(inp["L0_attn_W_out"]).reshape(2048, 2048)
    im["b_in_col"] = _bias_cols(inp["L0_attn_b_in"])
    im["b_dec_col"] = _bias_cols(inp["L0_attn_b_decay"])
    im["b_gate_col"] = _bias_cols(inp["L0_attn_b_gate"])
    im["b_out_col"] = _bias_cols(inp["L0_attn_b_out"])
    im["dec_bias_col"] = _bias_cols(inp["L0_attn_decay_bias"])
    im["wg_img"] = np.concatenate([_img_lhsT(inp["L0_ffn_Wg"][e]).reshape(2048, 2048)
                                   for e in range(8)], axis=0)
    im["wu_img"] = np.concatenate([_img_lhsT(inp["L0_ffn_Wu"][e]).reshape(2048, 2048)
                                   for e in range(8)], axis=0)
    im["wd_raw"] = np.ascontiguousarray(inp["L0_ffn_Wd"].reshape(8 * 2048, 2048))
    im["w0_img"] = _img_lhsT(np.ascontiguousarray(inp["L1_ffn_W0"].T)).reshape(8192, 2048)
    im["w1_img"] = _img_lhsT(np.ascontiguousarray(inp["L1_ffn_W1"].T)).reshape(2048, 8192)
    im["b0_col"] = _bias_cols(inp["L1_ffn_bias0"])
    im["b1_col"] = _bias_cols(inp["L1_ffn_bias1"])
    WinT = np.ascontiguousarray(inp["L1_attn_W_in"].T)
    WinT = np.concatenate([WinT, np.zeros((2048, 64), np.float32)], axis=1)
    im["win1_img"] = _img_lhsT(WinT).reshape(67 * 128, 2048)
    wc = inp["L1_attn_W_conv"].reshape(4352, 4)
    im["taps"] = np.ascontiguousarray(wc.reshape(34, 128, 4).transpose(1, 0, 2).reshape(128, 136))
    im["bconv"] = _bias_cols(inp["L1_attn_b_conv"])
    im["dtb"] = np.ascontiguousarray(inp["L1_attn_dt_bias"][:, None])
    im["alog"] = np.ascontiguousarray(inp["L1_attn_A_log"][:, None])
    im["dp_hp"] = _bias_cols(np.repeat(inp["L1_attn_Dp"], 64))
    im["wn_hp"] = _bias_cols(inp["L1_attn_norm_w"])
    im["wout1_img"] = _img_lhsT(np.ascontiguousarray(inp["L1_attn_W_out"].T)).reshape(2048, 4096)
    for k in ["wg_img", "wu_img", "wd_raw", "win_img", "wdec_img", "wgate_img",
              "wout_img", "w0_img", "w1_img", "win1_img", "wout1_img"]:
        im[k] = np.ascontiguousarray(im[k].astype(BF))
    return im


def _run(pg, maps):
    res = run_bass_kernel_spmd(pg.nc, maps, core_ids=list(range(NC)))
    return res.results


def kernel_multilaunch(**inputs):
    inp = {k: np.ascontiguousarray(np.asarray(v)) for k, v in inputs.items()}
    pr = _progs()
    im = _host_images(inp)
    x = inp["L0_attn_x"]
    xT = [np.ascontiguousarray(x[b].T) for b in range(NC)]
    C = {k: im["c_" + k] for k in
         ["ones_bf", "ones_f32", "ident_bf", "ident_f32", "triu01", "sel", "sel64"]}
    Cm = {"c_" + k: v for k, v in C.items()}

    r0 = _run(pr["s0"], [dict(Cm, xT=xT[b], wr_img=im["wr_img"]) for b in range(NC)])
    wd_img = np.concatenate([_img_lhsT(inp["L0_ffn_Wd"][e]).reshape(2048, 2048)
                             for e in range(8)], axis=0)
    r1 = _run(pr["s1"], [dict(Cm, xn_bf=r0[b]["xn_bf"], w_fm=r0[b]["w_fm"],
                              wg_img=im["wg_img"], wu_img=im["wu_img"], wd_img=wd_img)
                         for b in range(NC)])
    r2 = _run(pr["s2"], [dict(Cm, xn_bf=r0[b]["xn_bf"], xT=xT[b], g_bf=r1[b]["g_bf"],
                              win_img=im["win_img"], wdec_img=im["wdec_img"],
                              wgate_img=im["wgate_img"], wout_img=im["wout_img"],
                              b_in_col=im["b_in_col"], b_dec_col=im["b_dec_col"],
                              b_gate_col=im["b_gate_col"], b_out_col=im["b_out_col"],
                              dec_bias_col=im["dec_bias_col"]) for b in range(NC)])
    r3 = _run(pr["s3"], [dict(Cm, x1=r2[b]["x1"]) for b in range(NC)])
    r4 = _run(pr["s4"], [dict(Cm, xn2_bf=r3[b]["xn2_bf"], w0_img=im["w0_img"],
                              w1_img=im["w1_img"], b0_col=im["b0_col"], b1_col=im["b1_col"])
                         for b in range(NC)])
    r5 = _run(pr["s5"], [dict(Cm, xn2_bf=r3[b]["xn2_bf"], win_img=im["win1_img"],
                              taps=im["taps"], bconv=im["bconv"], dtb=im["dtb"],
                              alog=im["alog"]) for b in range(NC)])
    r6 = _run(pr["s6"], [dict(Cm, xBC_bf=r5[b]["xBC_bf"], z_silu=r5[b]["z_silu"],
                              A_arr=r5[b]["A_arr"], dp_hp=im["dp_hp"], wn_hp=im["wn_hp"])
                         for b in range(NC)])
    r7 = _run(pr["s7"], [dict(Cm, y_n=r6[b]["y_n"], wout1_img=im["wout1_img"],
                              g2_bf=r4[b]["g2_bf"], x1=r2[b]["x1"]) for b in range(NC)])
    out = np.stack([np.ascontiguousarray(r7[b]["outT"].T) for b in range(NC)], axis=0)
    return out



def build_fused():
    p = P()
    io = {}
    xT = p.dram("xT", [2048, S], F32, "ExternalInput")
    wr_img = p.dram("wr_img", [128, 128], F32, "ExternalInput")
    wg = p.dram("wg_img", [8 * 2048, 2048], BF16, "ExternalInput")
    wu = p.dram("wu_img", [8 * 2048, 2048], BF16, "ExternalInput")
    wd = p.dram("wd_raw", [8 * 2048, 2048], BF16, "ExternalInput")
    imgs0 = {k: p.dram(k + "_img", [2048, 2048], BF16, "ExternalInput")
             for k in ["win", "wdec", "wgate", "wout"]}
    bias0 = {k: p.dram(k + "_col", [128, 16], F32, "ExternalInput")
             for k in ["b_in", "b_dec", "b_gate", "b_out", "dec_bias"]}
    w0 = p.dram("w0_img", [8192, 2048], BF16, "ExternalInput")
    w1 = p.dram("w1_img", [2048, 8192], BF16, "ExternalInput")
    b0 = p.dram("b0_col", [128, 64], F32, "ExternalInput")
    b1 = p.dram("b1_col", [128, 16], F32, "ExternalInput")
    win1 = p.dram("win1_img", [67 * 128, 2048], BF16, "ExternalInput")
    taps = p.dram("taps", [128, 34 * 4], F32, "ExternalInput")
    bconv = p.dram("bconv", [128, 34], F32, "ExternalInput")
    dtb = p.dram("dtb", [64, 1], F32, "ExternalInput")
    alog = p.dram("alog", [64, 1], F32, "ExternalInput")
    dp = p.dram("dp_hp", [128, 32], F32, "ExternalInput")
    wn = p.dram("wn_hp", [128, 32], F32, "ExternalInput")
    wout1 = p.dram("wout1_img", [2048, 4096], BF16, "ExternalInput")
    out = p.dram("outT", [2048, S], F32, "ExternalOutput")
    xn = p.dram("i_xn", [2048, S], BF16, "Internal")
    w_fm = p.dram("i_wfm", [8, S], F32, "Internal")
    g_bf = p.dram("i_g", [2048, S], BF16, "Internal")
    x1 = p.dram("i_x1", [2048, S], F32, "Internal")
    xn2 = p.dram("i_xn2", [2048, S], BF16, "Internal")
    g2 = p.dram("i_g2", [2048, S], BF16, "Internal")
    z = p.dram("i_z", [4096, S], BF16, "Internal")
    xbc = p.dram("i_xbc", [34 * 128, S], BF16, "Internal")
    a_arr = p.dram("i_a", [64, S], F32, "Internal")
    yn = p.dram("i_yn", [4096, S], BF16, "Internal")
    r_off = p.dram("i_roff", [128, 32], mybir.dt.int32, "Internal")
    r_w = p.dram("i_rw", [128, 32], F32, "Internal")
    cpx = p.dram("i_cpx", [8 * 768, 2048], BF16, "Internal")
    wcp = p.dram("i_wcp", [8 * 768, 1], F32, "Internal")
    bko = p.dram("i_bko", [8 * 768, 1], mybir.dt.int32, "Internal")
    pab = p.dram("i_pab", [2 * S, 2048], BF16, "Internal")
    io["r_off_ap"] = r_off
    io["r_w_ap"] = r_w

    load_consts(p, io)
    bar = p.sem("bar")
    with p.scope():
        emit_ln_router(p, io, "s0", xT, xn, router=True, wr_img_ap=wr_img, w_ap=w_fm)
    p.barrier(bar)
    with p.scope():
        emit_moe_sparse(p, io, "s1", xn, r_off, r_w, wg, wu, wd, g_bf,
                        cpx, wcp, bko, pab)
    p.barrier(bar)
    with p.scope():
        emit_l0_attn(p, io, "s2", xn, xT, g_bf, x1, imgs0, bias0)
    p.barrier(bar)
    with p.scope():
        emit_ln_router(p, io, "s3", x1, xn2, router=False)
    p.barrier(bar)
    with p.scope():
        emit_ffn(p, io, "s4", xn2, w0, w1, b0, b1, g2)
    p.barrier(bar)
    with p.scope():
        emit_zxbcdt(p, io, "s5", xn2, win1, taps, bconv, dtb, alog, z, xbc, a_arr)
    p.barrier(bar)
    with p.scope():
        emit_ssd(p, io, "s6", xbc, z, a_arr, dp, wn, yn)
    p.barrier(bar)
    with p.scope():
        emit_l1_out(p, io, "s7", yn, wout1, g2, x1, out)
    p.close()
    return p


_FUSED = []


def kernel_fused(**inputs):
    inp = {k: np.ascontiguousarray(np.asarray(v)) for k, v in inputs.items()}
    if not _FUSED:
        _FUSED.append(build_fused())
    pg = _FUSED[0]
    im = _host_images(inp)
    x = inp["L0_attn_x"]
    maps = []
    for b in range(NC):
        m = {k: v for k, v in im.items()}
        m["xT"] = np.ascontiguousarray(x[b].T)
        maps.append(m)
    res = run_bass_kernel_spmd(pg.nc, maps, core_ids=list(range(NC)))
    return np.stack([np.ascontiguousarray(res.results[b]["outT"].T) for b in range(NC)], axis=0)


kernel = kernel_fused



def emit_moe_sparse(p, io, sfx, xn_ap, roff_ap, rw_ap, wg_img, wu_img, wd_raw,
                    g_ap, cpx_ap, wcp_ap, bko_ap, pab_ap):
    from concourse.bass_core_rust import IndirectOffsetOnAxis
    nc = p.nc
    g_, v, t, s_, sy = nc.gpsimd, nc.vector, nc.tensor, nc.scalar, nc.sync
    C = 768
    NE = 8

    xs = p.sbuf(sfx + "_xs", [128, 2 * 2048], BF16)
    tokb = p.sbuf(sfx + "_tok", [128, 2 * 2048], BF16)
    offs = p.sbuf(sfx + "_offs", [128, 32], mybir.dt.int32)
    w12 = p.sbuf(sfx + "_w12", [128, 32], F32)
    bkv = p.sbuf(sfx + "_bkv", [128, 32], mybir.dt.int32)
    bkvf = p.sbuf(sfx + "_bkvf", [128, 2], F32)
    ibuf = p.sbuf(sfx + "_ibuf", [128, 48], mybir.dt.int32)
    cxf = p.sbuf(sfx + "_cxf", [128, 16 * C], BF16)
    hp = p.sbuf(sfx + "_hp", [128, 16 * C], BF16)
    wdbuf = p.sbuf(sfx + "_wd", [128, 16 * 2048], BF16)
    wrow = p.sbuf(sfx + "_wrow", [1, C], F32)
    wb = p.sbuf(sfx + "_wb2", [128, C], BF16)
    tt = p.sbuf(sfx + "_tt", [128, 2 * 384], BF16)
    tko = p.sbuf(sfx + "_tko", [128, 2 * 2048], BF16)
    bks = p.sbuf(sfx + "_bks", [128, 48], mybir.dt.int32)
    gf = p.sbuf(sfx + "_gf", [128, 2048], BF16)
    ga = p.sbuf(sfx + "_ga", [128, 2 * 2048], BF16)
    gb2 = p.sbuf(sfx + "_gb2", [128, 2 * 2048], BF16)

    pool = Pool(p, sfx, max_nk=16, cbw=384, dual=True)
    psD = p.psum([128, 1024])
    psT = p.psum([128, 1024], BF16)

    ld = p.sem(sfx + "_ld")
    bld = p.sem(sfx + "_bld")
    tv = p.sem(sfx + "_tv")
    vv = p.sem(sfx + "_vv")
    sc = p.sem(sfx + "_sc")
    gsc = p.sem(sfx + "_gsc")
    wld = p.sem(sfx + "_wld")
    st = p.sem(sfx + "_st")

    ones_f32 = io["ones_f32_sb"]
    ident_bf = io["ident_bf_sb"]
    iota128 = io["iota128_sb"]
    p.wait(g_, io["c_ld"])
    p.wait(t, io["c_ld"])
    p.wait(sy, io["c_ld"])

    p.inc(g_.dma_start(out=offs[:, :], in_=roff_ap), bld, 16)
    p.inc(g_.dma_start(out=w12[:, :], in_=rw_ap), bld, 16)
    zf32 = p.sbuf(sfx + "_zf32", [128, 48], F32)
    p.wait(v, bld)
    p.wait(v, io["c_ld"])
    v.memset(ibuf[:, :], 100000)
    v.memset(zf32[:, :], 0.0)
    v.drain()
    for gi in range(16):
        v.tensor_scalar_add(bkvf[:, 0:1], iota128[:, 0:1], float(gi * 128))
        v.drain()
        v.tensor_scalar_add(bkvf[:, 1:2], bkvf[:, 0:1], float(S))
        v.drain()
        v.tensor_copy(bkv[:, gi:gi + 1], bkvf[:, 0:1])
        i0 = v.tensor_copy(bkv[:, 16 + gi:17 + gi], bkvf[:, 1:2])
        v.drain()
    p.inc(i0, vv)
    p.wait(g_, vv)
    p.inc(g_.dma_start(out=bko_ap, in_=ibuf[:, :]), gsc, 16)
    p.inc(g_.dma_start(out=wcp_ap, in_=zf32[:, :]), gsc, 16)
    bko_init = p.val(gsc)

    ph_vv = []
    phA_tv = []
    gsc_g = []
    for gi in range(16):
        par = gi % 2
        if gi >= 2:
            p.wait(sy, tv, phA_tv[gi - 2])
        p.inc(sy.dma_start(out=xs[:, par * 2048:(par + 1) * 2048]
                           .rearrange("p (j c) -> p j c", j=16),
                           in_=xn_ap[:, gi * 128:(gi + 1) * 128]
                           .rearrange("(j p) c -> p j c", p=128)), ld, 16)
        p.wait(t, ld, p.val(ld))
        if gi >= 1:
            p.wait(t, vv, ph_vv[-1])
        for r2 in range(2):
            for i in range(8):
                i1 = t.transpose(psT[:, i * 128:(i + 1) * 128],
                                 xs[:, par * 2048 + (r2 * 8 + i) * 128: par * 2048 + (r2 * 8 + i + 1) * 128],
                                 ident_bf[:, :])
            p.inc(i1, tv)
            p.wait(v, tv)
            if gi >= 2 and r2 == 0:
                p.wait(v, gsc, gsc_g[gi - 2])
            i2 = v.tensor_copy(tokb[:, par * 2048 + r2 * 1024: par * 2048 + (r2 + 1) * 1024],
                               psT[:, :])
            p.inc(i2, vv)
            p.wait(t, vv)
        phA_tv.append(p.val(tv))
        ph_vv.append(p.val(vv))
        p.wait(g_, vv)
        if gi == 0:
            p.wait(g_, gsc, bko_init)
            p.wait(g_, bld)
        tk = tokb[:, par * 2048:(par + 1) * 2048]
        for rank in range(2):
            oc = offs[:, rank * 16 + gi: rank * 16 + gi + 1]
            i3 = g_.indirect_dma_start(cpx_ap, IndirectOffsetOnAxis(ap=oc, axis=0), tk, None)
            p.inc(i3, gsc, 16)
            i3 = g_.indirect_dma_start(wcp_ap, IndirectOffsetOnAxis(ap=oc, axis=0),
                                       w12[:, rank * 16 + gi: rank * 16 + gi + 1], None)
            p.inc(i3, gsc, 16)
            i3 = g_.indirect_dma_start(bko_ap, IndirectOffsetOnAxis(ap=oc, axis=0),
                                       bkv[:, rank * 16 + gi: rank * 16 + gi + 1], None)
            p.inc(i3, gsc, 16)
        gsc_g.append(p.val(gsc))
    tvA = p.val(tv)
    p.wait(sy, gsc, p.val(gsc))
    for ci in range(48):
        p.inc(sy.dma_start(out=bks[:, ci:ci + 1],
                           in_=bko_ap[ci * 128:(ci + 1) * 128, :]), bld, 16)

    if io.get("s1s_phases") == "A":
        p.wait(sy, gsc, p.val(gsc))
        p.inc(sy.dma_start(out=io["dbg_cpx"], in_=cpx_ap), st, 16)
        p.inc(sy.dma_start(out=io["dbg_wcp"], in_=wcp_ap), st, 16)
        p.inc(sy.dma_start(out=io["dbg_bko"], in_=bko_ap), st, 16)
        p.wait(sy, st)
        return

    dbgbar = io.get("s1s_bar")
    nexp = io.get("s1s_nexp", NE)
    gst_tv = []
    prev_wb_tv = None
    for e in range(nexp):
        if dbgbar:
            p.barrier(dbgbar)
        if e == 0:
            p.wait(g_, gsc, bko_init)
        if e > 0:
            p.wait(g_, tv, p.val(tv))
        for k in range(16):
            p.inc(g_.dma_start(out=wdbuf[:, k * 2048:(k + 1) * 2048],
                               in_=wd_raw[e * 2048 + k * 128: e * 2048 + (k + 1) * 128, :]), wld, 16)
        wld_e = p.val(wld)
        if e > 0:
            p.wait(sy, tv, prev_wb_tv)
        p.inc(sy.dma_start(out=wrow[0:1, :],
                           in_=wcp_ap[e * C:(e + 1) * C, :].rearrange("c x -> x c")), bld, 16)
        p.wait(t, bld)
        if e > 0:
            p.wait(t, vv, p.val(vv))
        t.matmul(psD[:, 0:512], ones_f32[0:1, :], wrow[0:1, 0:512], start=True, stop=True)
        i4 = t.matmul(psD[:, 512:768], ones_f32[0:1, :], wrow[0:1, 512:768], start=True, stop=True)
        p.inc(i4, tv)
        prev_wb_tv = p.val(tv)
        p.wait(v, tv)
        i5 = v.tensor_copy(wb[:, 0:512], psD[:, 0:512])
        i5 = v.tensor_copy(wb[:, 512:768], psD[:, 512:768])
        p.inc(i5, vv)
        wb_vv = p.val(vv)
        for ct in range(6):
            gi2 = e * 6 + ct
            gpar = gi2 % 2
            p.wait(sy, tv, gst_tv[gi2 - 2] if gi2 >= 2 else tvA)
            p.inc(sy.dma_start(out=xs[:, gpar * 2048:(gpar + 1) * 2048],
                               in_=cpx_ap[e * C + ct * 128: e * C + (ct + 1) * 128, :]), ld, 16)
            p.wait(t, ld, p.val(ld))
            p.wait(t, vv, p.val(vv))
            for r2 in range(2):
                for i in range(8):
                    i6 = t.transpose(psT[:, i * 128:(i + 1) * 128],
                                     xs[:, gpar * 2048 + (r2 * 8 + i) * 128:
                                         gpar * 2048 + (r2 * 8 + i + 1) * 128],
                                     ident_bf[:, :])
                p.inc(i6, tv)
                p.wait(v, tv)
                for i in range(8):
                    j = r2 * 8 + i
                    i7 = v.tensor_copy(cxf[:, j * C + ct * 128: j * C + (ct + 1) * 128],
                                       psT[:, i * 128:(i + 1) * 128])
                p.inc(i7, vv)
                p.wait(t, vv)
            gst_tv.append(p.val(tv))
        cxf_vv = p.val(vv)

        if io.get("s1s_skip_mm"):
            continue
        def evacA(mi, cb, pa, pb, e=e):
            n = pool.n_group - 1
            par = n % 2
            p.wait(s_, pool.mm)
            if p.val(pool.bk) >= 2:
                p.wait(s_, pool.bk, p.val(pool.bk) - 1)
            i1 = s_.activation(tt[:, par * 384:(par + 1) * 384], pa, AF.Silu)
            p.inc(i1, sc)
            p.wait(v, sc)
            p.wait(v, vv, wb_vv)
            dsl = hp[:, mi * C + cb * 384: mi * C + (cb + 1) * 384]
            v.tensor_mul(dsl, tt[:, par * 384:(par + 1) * 384], pb)
            v.drain()
            i2 = v.tensor_mul(dsl, dsl, wb[:, cb * 384:(cb + 1) * 384])
            p.inc(i2, pool.bk)
            return 1

        mm_pass(p, pool, src_sb=cxf, src_cols=C, n_k=16, n_m=16,
                Wimg=wg_img[e * 2048:(e + 1) * 2048, :],
                W2img=wu_img[e * 2048:(e + 1) * 2048, :],
                evac=evacA, src_wait=(vv, cxf_vv))

        if io.get("s1s_skip_dp"):
            continue
        p.wait(t, pool.bk)
        p.wait(t, wld, wld_e)
        dp_vv = []
        for ct in range(6):
            ctp = ct % 2
            if dbgbar:
                p.barrier(dbgbar)
            if ct >= 2:
                p.wait(t, gsc, p.val(gsc))
            for qt in range(4):
                qp = qt % 2
                qi = ct * 4 + qt
                if qi >= 2:
                    p.wait(t, vv, dp_vv[qi - 2])
                else:
                    p.wait(t, vv, wb_vv)
                for k in range(16):
                    i8 = t.matmul(psD[:, qp * 512:(qp + 1) * 512],
                                  hp[:, k * C + ct * 128: k * C + (ct + 1) * 128],
                                  wdbuf[:, k * 2048 + qt * 512:
                                        k * 2048 + (qt + 1) * 512],
                                  start=(k == 0), stop=(k == 15))
                p.inc(i8, tv)
                p.wait(v, tv)
                i9 = v.tensor_copy(tko[:, ctp * 2048 + qt * 512: ctp * 2048 + (qt + 1) * 512],
                                   psD[:, qp * 512:(qp + 1) * 512])
                p.inc(i9, vv)
                dp_vv.append(p.val(vv))
            p.wait(g_, vv)
            p.wait(g_, bld)
            i10 = g_.indirect_dma_start(
                pab_ap, IndirectOffsetOnAxis(ap=bks[:, e * 6 + ct: e * 6 + ct + 1], axis=0),
                tko[:, ctp * 2048:(ctp + 1) * 2048], None)
            p.inc(i10, gsc, 16)

    if io.get("s1s_phases") == "B":
        p.wait(sy, gsc, p.val(gsc))
        p.inc(sy.dma_start(out=io["dbg_pab"], in_=pab_ap), st, 16)
        p.inc(sy.dma_start(out=io["dbg_bko2"], in_=bko_ap), st, 16)
        p.inc(sy.dma_start(out=io["dbg_cpx2"], in_=cpx_ap), st, 16)
        p.inc(sy.dma_start(out=io["dbg_hp"], in_=hp[:, :]), st, 16)
        p.inc(sy.dma_start(out=io["dbg_tko"], in_=tko[:, :]), st, 16)
        p.inc(sy.dma_start(out=io["dbg_wd"], in_=wdbuf[:, :]), st, 16)
        p.wait(sy, st)
        return

    p.wait(sy, gsc, p.val(gsc))
    for gi in range(16):
        par = gi % 2
        if gi >= 2:
            p.wait(sy, st, p.val(st))
        p.inc(sy.dma_start(out=ga[:, par * 2048:(par + 1) * 2048],
                           in_=pab_ap[gi * 128:(gi + 1) * 128, :]), ld, 16)
        p.inc(sy.dma_start(out=gb2[:, par * 2048:(par + 1) * 2048],
                           in_=pab_ap[S + gi * 128: S + (gi + 1) * 128, :]), ld, 16)
        p.wait(v, ld, p.val(ld))
        v.drain()
        i11 = v.tensor_add(ga[:, par * 2048:(par + 1) * 2048],
                           ga[:, par * 2048:(par + 1) * 2048],
                           gb2[:, par * 2048:(par + 1) * 2048])
        p.inc(i11, vv)
        p.wait(s_, vv)
        i12 = s_.activation(gb2[:, par * 2048:(par + 1) * 2048],
                            ga[:, par * 2048:(par + 1) * 2048], AF.Silu)
        p.inc(i12, sc)
        p.wait(t, sc)
        if gi >= 1:
            p.wait(v, st, p.val(st))
        for r2 in range(2):
            p.wait(t, vv, p.val(vv))
            for i in range(8):
                i13 = t.transpose(psT[:, i * 128:(i + 1) * 128],
                                  gb2[:, par * 2048 + (r2 * 8 + i) * 128:
                                      par * 2048 + (r2 * 8 + i + 1) * 128],
                                  ident_bf[:, :])
            p.inc(i13, tv)
            p.wait(v, tv)
            i14 = v.tensor_copy(gf[:, r2 * 1024:(r2 + 1) * 1024], psT[:, :])
            p.inc(i14, vv)
        p.wait(sy, vv)
        p.inc(sy.dma_start(out=g_ap[:, gi * 128:(gi + 1) * 128]
                           .rearrange("(j p) c -> p j c", p=128),
                           in_=gf[:, :].rearrange("p (j c) -> p j c", j=16)), st, 16)
    p.wait(sy, st)
    p.wait(sy, gsc, p.val(gsc))


def build_s1s(phases=None, bar=False):
    p = P()
    io = {}
    if bar:
        io["s1s_bar"] = p.sem("s1s_dbgbar")
    if phases == "A":
        io["s1s_phases"] = "A"
        io["dbg_cpx"] = p.dram("dbg_cpx", [8 * 768, 2048], BF16, "ExternalOutput")
        io["dbg_wcp"] = p.dram("dbg_wcp", [8 * 768, 1], F32, "ExternalOutput")
        io["dbg_bko"] = p.dram("dbg_bko", [8 * 768, 1], mybir.dt.int32, "ExternalOutput")
    if phases is not None and phases.startswith("B"):
        io["s1s_phases"] = "B"
        io["dbg_pab"] = p.dram("dbg_pab", [2 * S, 2048], BF16, "ExternalOutput")
        io["dbg_bko2"] = p.dram("dbg_bko2", [8 * 768, 1], mybir.dt.int32, "ExternalOutput")
        io["dbg_cpx2"] = p.dram("dbg_cpx2", [8 * 768, 2048], BF16, "ExternalOutput")
        io["dbg_hp"] = p.dram("dbg_hp", [128, 16 * 768], BF16, "ExternalOutput")
        io["dbg_tko"] = p.dram("dbg_tko", [128, 2 * 2048], BF16, "ExternalOutput")
        io["dbg_wd"] = p.dram("dbg_wd", [128, 16 * 2048], BF16, "ExternalOutput")
        if phases == "B1":
            io["s1s_nexp"] = 1
        if phases == "B0":
            io["s1s_nexp"] = 1
            io["s1s_skip_dp"] = True
        if phases == "B00":
            io["s1s_nexp"] = 1
            io["s1s_skip_dp"] = True
            io["s1s_skip_mm"] = True
    xn = p.dram("xn_bf", [2048, S], BF16, "ExternalInput")
    roff = p.dram("r_off", [128, 32], mybir.dt.int32, "ExternalInput")
    rw = p.dram("r_w", [128, 32], F32, "ExternalInput")
    wg = p.dram("wg_img", [8 * 2048, 2048], F32, "ExternalInput")
    wu = p.dram("wu_img", [8 * 2048, 2048], F32, "ExternalInput")
    wd = p.dram("wd_raw", [8 * 2048, 2048], F32, "ExternalInput")
    gb = p.dram("g_bf", [2048, S], BF16, "ExternalOutput")
    cpx = p.dram("i_cpx", [8 * 768, 2048], BF16, "Internal")
    wcp = p.dram("i_wcp", [8 * 768, 1], F32, "Internal")
    bko = p.dram("i_bko", [8 * 768, 1], mybir.dt.int32, "Internal")
    pab = p.dram("i_pab", [2 * S, 2048], BF16, "Internal")
    load_consts(p, io)
    emit_moe_sparse(p, io, "s1s", xn, roff, rw, wg, wu, wd, gb, cpx, wcp, bko, pab)
    p.close()
    return p
